# version 05
# baseline (speedup 1.0000x reference)
import os
import numpy as np
import ml_dtypes
import concourse.bass as bass
import concourse.mybir as mybir
from concourse.bass_utils import run_bass_kernel_spmd

F32 = mybir.dt.float32
BF16 = mybir.dt.bfloat16
AF = mybir.ActivationFunctionType
ALU = mybir.AluOpType
AX = mybir.AxisListType

B, S, E, H = 2, 4096, 2048, 16
D = 128
TOK = 1024
GW = 512
FC = 16
SCALE = float(1.0 / np.sqrt(128.0))
NEG = -1.0e9
MW = 897

bfd = ml_dtypes.bfloat16
N_SUB = int(os.environ.get("ATH_SUBLAYERS", "4"))
ENGS = ("tensor", "vector", "scalar", "sync", "gpsimd")


def tile_weight(w, kparts):
    K, M = w.shape
    t = w.reshape(kparts, 128, M // 128, 128).transpose(2, 0, 1, 3)
    return np.ascontiguousarray(t.astype(np.float32))


def rope_tables_for(positions):
    inv = 1.0 / (10000.0 ** (np.arange(0, D, 2, dtype=np.float64) / D))
    ang = np.concatenate([np.outer(positions, inv)] * 2, axis=1)
    cos = np.cos(ang).T
    sin = np.sin(ang).T
    sgn = np.where(np.arange(D) % 2 == 0, -1.0, 1.0)[:, None]
    return (np.ascontiguousarray(cos).astype(bfd),
            np.ascontiguousarray(sin * sgn).astype(bfd))


def masks_for(lane):
    out = np.empty((4, 128, MW), np.float32)
    kap = np.arange(128)[:, None]
    wc = np.arange(MW)[None, :]
    for r in range(4):
        dlt = 0 if r <= lane else -1
        out[r] = np.where(kap <= wc - 385 + dlt, 0.0, NEG)
    return np.ascontiguousarray(out.reshape(512, MW)).astype(bfd)


def host_inputs(inp):
    f32 = lambda a: np.asarray(a, np.float32)
    shared = {
        "wqa": tile_weight(np.ascontiguousarray(f32(inp["L0_attn_a_Wqkv"]).T), FC),
        "woa": tile_weight(np.ascontiguousarray(f32(inp["L0_attn_a_Wout"]).T), FC),
        "wqb": tile_weight(f32(inp["L0_attn_b_W_qkv"]), FC),
        "wob": tile_weight(f32(inp["L0_attn_b_W_out"]), FC),
        "wg": tile_weight(f32(inp["L0_ffn_w_gate"]), FC),
        "wu": tile_weight(f32(inp["L0_ffn_w_up"]), FC),
        "wd": tile_weight(f32(inp["L0_ffn_w_down"]), 8),
        "wsg": tile_weight(f32(inp["L0_ffn_shared_w_gate"]), FC),
        "wsu": tile_weight(f32(inp["L0_ffn_shared_w_up"]), FC),
        "wsd": tile_weight(f32(inp["L0_ffn_shared_w_down"]), FC),
        "wrt": np.ascontiguousarray(f32(inp["L0_ffn_w_router"]).reshape(FC, 128, 128)),
    }
    cvec = np.zeros((128, 256), np.float32)
    cvec[:, 0:128] = np.eye(128, dtype=np.float32)
    cvec[:, 128:176] = f32(inp["L0_attn_b_b_qkv"]).reshape(48, 128).T
    cvec[:, 176:192] = f32(inp["L0_attn_b_b_out"]).reshape(16, 128).T
    cvec[:, 192:193] = f32(inp["L0_ffn_expert_scale"]).reshape(128, 1)
    cvec[:, 193:201] = f32(inp["L0_ffn_amplitudes"])[..., 0]
    cvec[:, 201:209] = f32(inp["L0_ffn_amplitudes"])[..., 1]
    shared["cvec"] = cvec
    shared["identb"] = np.eye(128, dtype=np.float32).astype(bfd)

    x = f32(inp["L0_attn_a_x"])
    maps = []
    for c in range(8):
        beta, lane = c // 4, c % 4
        pos = np.arange(lane, S, 4)
        m = dict(shared)
        m["x"] = np.ascontiguousarray(x[beta, pos, :].T)
        m["cosb"], m["sinb"] = rope_tables_for(pos)
        m["masks"] = masks_for(lane)
        maps.append(m)
    return maps


def build_nc():
    import contextlib
    nc = bass.Bass("TRN2", target_bir_lowering=False,
                   detect_race_conditions=False)
    ctx = contextlib.ExitStack()

    d_x = nc.dram_tensor("x", [2048, TOK], F32, kind="ExternalInput").ap()
    d_cos = nc.dram_tensor("cosb", [128, TOK], BF16, kind="ExternalInput").ap()
    d_sin = nc.dram_tensor("sinb", [128, TOK], BF16, kind="ExternalInput").ap()
    d_msk = nc.dram_tensor("masks", [512, MW], BF16, kind="ExternalInput").ap()
    d_cvec = nc.dram_tensor("cvec", [128, 256], F32, kind="ExternalInput").ap()
    d_idb = nc.dram_tensor("identb", [128, 128], BF16, kind="ExternalInput").ap()
    d_wrt = nc.dram_tensor("wrt", [FC, 128, 128], F32, kind="ExternalInput").ap()
    dw = {}
    for nm, mt, kc in (("wqa", 48, FC), ("woa", 16, FC), ("wqb", 48, FC),
                       ("wob", 16, FC), ("wg", 64, FC), ("wu", 64, FC),
                       ("wd", 16, 8), ("wsg", 16, FC), ("wsu", 16, FC),
                       ("wsd", 16, FC)):
        dw[nm] = nc.dram_tensor(nm, [mt, kc, 128, 128], F32,
                                kind="ExternalInput").ap()
    d_out = nc.dram_tensor("out", [2048, TOK], F32, kind="ExternalOutput").ap()
    cc_in = nc.dram_tensor("cc_in", [4096, TOK], BF16, kind="Internal").ap()
    cc_out = nc.dram_tensor("cc_out", [16384, TOK], BF16, kind="Internal").ap()

    sbt = lambda n, s, d: ctx.enter_context(nc.sbuf_tensor(n, s, d))

    xb = sbt("xb", [128, FC * TOK], F32)
    hb = sbt("hb", [128, FC * TOK], BF16)
    u1 = sbt("u1", [128, FC * TOK], BF16)
    u2 = sbt("u2", [128, FC * TOK], BF16)
    wbuf = sbt("wbuf", [128, 4096], BF16)
    cf = sbt("cf", [128, 256], F32)
    cbi = sbt("cbi", [128, 128], BF16)
    wrt_hl = sbt("wrt_hl", [128, 4096], BF16)
    cosb = sbt("cosb_s", [128, TOK], BF16)
    sinb = sbt("sinb_s", [128, TOK], BF16)
    sc32 = sbt("sc32", [128, 3712], F32)
    scbf = sbt("scbf", [128, 2048], BF16)
    rows = sbt("rows", [128, 1024], F32)
    mixb = sbt("mixb", [128, 8], F32)
    onesr = sbt("onesr", [1, 128], F32)
    ones32 = sbt("ones32", [128, 1], F32)
    onesb16 = sbt("onesb16", [128, 1], BF16)
    epsb = sbt("epsb", [128, 1], F32)

    U1_Q = 0
    U1_ACC = 0
    U1_CK = 8192
    U2_K = 0
    U2_EXP = 2048
    U2_V = 4096
    U2_MASK = 6144
    U2_VS = 9984
    U2_VT0 = 11008
    U2_VT1 = 12032
    U2_KV = 13056
    U2_RT = 14080
    SC_XSQ, SC_T, SC_H32 = 0, 512, 1024
    SC_PROB = 1536
    SC_WFT = 2560
    R_MEAN, R_T1, R_VAR = (0, 0), (32, 0), (64, 0)
    R_SD, R_RSTD = (32, 512), (0, 512)

    def row(slot, w=512):
        p, c0 = slot
        return rows[p:p + 1, c0:c0 + w]

    psA = ctx.enter_context(nc.psum_tensor("psA", [128, 1024], F32))
    psB = ctx.enter_context(nc.psum_tensor("psB", [128, 1024], F32))
    psAV1 = ctx.enter_context(nc.psum_tensor("psAV1", [128, 512], F32))
    psAV2 = ctx.enter_context(nc.psum_tensor("psAV2", [128, 512], F32))
    psX = ctx.enter_context(nc.psum_tensor("psX", [128, 1024], F32))
    psS1 = psX[0:1, 0:512]
    psS2 = psX[0:1, 512:1024]
    psXb = psX[:].bitcast(BF16)
    psT = [psXb[:, 0:128], psXb[:, 1024:1152]]
    GEN = [psA[:, 0:512], psA[:, 512:1024], psB[:, 0:512], psB[:, 512:1024]]

    sems = {}
    C = {}
    for s in ("sW", "sWf", "sH", "sP1", "sP2", "sV", "sS", "sES", "sC", "sC2"):
        sems[s] = ctx.enter_context(nc.semaphore(s))
        C[s] = 0

    def inc(name, n=1):
        C[name] += n
        return C[name]

    phases = []

    class Ph:
        def __init__(self, name):
            self.name = name
            self.ops = {e: [] for e in ENGS}
            phases.append(self)

        def add(self, eng, fn):
            self.ops[eng].append(fn)

        def wt(self, eng, sname, val):
            if val and val > 0:
                self.ops[eng].append(
                    lambda e, s=sems[sname], v=val: e.wait_ge(s, v))

        def op(self, eng, build, att=(), selfwait=None):
            rs = [(sems[n], a) for (n, a) in att]
            sw = (sems[selfwait[0]], selfwait[1]) if selfwait else None
            def run(e, b=build, rs=rs, sw=sw):
                ins = b(e)
                if rs:
                    ins.then_inc(*rs[0])
                    if len(rs) > 1 and sw is not None:
                        e.wait_ge(*sw)
                    for (h, a) in rs[1:]:
                        e.nop().then_inc(h, a)
            self.add(eng, run)

    wf_n = [0]

    def sched_strips(ph, striplist):
        recs = []
        for (nm, mi, kc) in striplist:
            wf_n[0] += 1
            recs.append(dict(nm=nm, mi=mi, kc=kc, slot=(wf_n[0] - 1) % 2,
                             sWf_need=wf_n[0] - 2, sW_after=inc("sW", 16)))
        for r in recs:
            ph.wt("gpsimd", "sWf", r["sWf_need"])
            dst = wbuf[:, r["slot"] * 2048: r["slot"] * 2048 + r["kc"] * 128]
            ph.op("gpsimd",
                  lambda g, d=dst.rearrange("p (k c) -> p k c", c=128),
                  s=dw[r["nm"]][r["mi"]].rearrange("k p c -> p k c"):
                  g.dma_start(out=d, in_=s),
                  att=[("sW", 16)])
        return recs

    def wslice(rec, kch):
        return wbuf[:, rec["slot"] * 2048 + kch * 128:
                    rec["slot"] * 2048 + kch * 128 + 128]

    def hslc(kc, g, w=512, off=0):
        return hb[:, kc * TOK + g * GW + off: kc * TOK + g * GW + off + w]

    def xslc(kc, g):
        return xb[:, kc * TOK + g * GW: kc * TOK + g * GW + 512]

    ph = Ph("init")
    loads = [
        (xb[:].rearrange("p (k t) -> p k t", t=TOK),
         d_x.rearrange("(k p) t -> p k t", p=128)),
        (cf[:], d_cvec), (cbi[:], d_idb), (cosb[:], d_cos), (sinb[:], d_sin),
        (u2[:, U2_MASK:U2_MASK + 4 * MW].rearrange("p (r w) -> p r w", w=MW),
         d_msk.rearrange("(r p) w -> p r w", p=128)),
        (sc32[:, 0:2048].rearrange("p (k c) -> p k c", c=128),
         d_wrt.rearrange("k p c -> p k c")),
    ]
    for dst, src in loads:
        inc("sH", 16)
        ph.op("sync", lambda s, d=dst, i=src: s.dma_start(out=d, in_=i),
              att=[("sH", 16)])
    ph.wt("sync", "sH", C["sH"])
    ph.wt("scalar", "sH", C["sH"])
    ph.wt("tensor", "sH", C["sH"])
    ph.wt("vector", "sH", C["sH"])
    ph.add("vector", lambda v: v.memset(onesr[:], 1.0))
    ph.add("vector", lambda v: v.memset(ones32[:], 1.0))
    ph.add("vector", lambda v: v.memset(onesb16[:], 1.0))
    ph.add("vector", lambda v: v.memset(epsb[:], 1e-5))

    ph = Ph("init2")
    amp0 = cf[:, 193:201]
    amp1 = cf[:, 201:209]
    sqa = sc32[:, 3616:3624]
    sqb = sc32[:, 3624:3632]
    p2 = sc32[:, 3632:3640]
    rs1 = sc32[:, 3640:3641]
    rcp1 = sc32[:, 3641:3642]
    s1 = inc("sS")
    s2 = inc("sS")
    ph.op("scalar", lambda sc: sc.activation(sqa, amp0, AF.Square),
          att=[("sS", 1)])
    ph.op("scalar", lambda sc: sc.activation(sqb, amp1, AF.Square),
          att=[("sS", 1)])
    ph.add("vector", lambda v: v.tensor_copy(wrt_hl[:, 0:2048], sc32[:, 0:2048]))
    ph.add("vector", lambda v: v.tensor_tensor(
        wrt_hl[:, 2048:4096], sc32[:, 0:2048], wrt_hl[:, 0:2048], ALU.subtract))
    ph.wt("vector", "sS", s2)
    v0 = inc("sV")
    ph.op("vector", lambda v: v.tensor_add(p2, sqa, sqb), att=[("sV", 1)])
    ph.wt("vector", "sV", v0)
    v1 = inc("sV")
    ph.op("vector", lambda v: v.reduce_sum(rs1, p2, axis=AX.X), att=[("sV", 1)])
    ph.wt("vector", "sV", v1)
    v1b = inc("sV")
    ph.op("vector", lambda v: v.tensor_scalar(rs1, rs1, 1e-8, None, ALU.add),
          att=[("sV", 1)])
    ph.wt("vector", "sV", v1b)
    v2 = inc("sV")
    ph.op("vector", lambda v: v.reciprocal(rcp1, rs1), att=[("sV", 1)])
    ph.wt("vector", "sV", v2)
    ph.add("vector", lambda v: v.tensor_scalar(mixb[:], p2, rcp1, None, ALU.mult))

    def emit_ln(tag, with_router):
        ph = Ph("ln_" + tag)
        vbar = C["sV"]
        ph.wt("scalar", "sV", vbar)
        ph.wt("tensor", "sV", vbar)
        pl = {}
        for g in range(2):
            pl[g] = dict(sq=[inc("sS") for _ in range(FC)], sqrt=inc("sS"))
        for g in range(2):
            pl[g]["mm"] = [inc("sP1") for _ in range(FC)]
            pl[g]["bcast"] = inc("sP1")
            if with_router:
                pl[g]["rmm"] = [inc("sP1") for _ in range(FC)]
        if with_router:
            pl["router_done"] = inc("sP1")
        for g in range(2):
            pl[g]["rows_a"] = inc("sV")
            pl[g]["rows_c"] = inc("sV")
            pl[g]["hn"] = [inc("sV") for _ in range(FC)]

        for g in range(2):
            p = pl[g]
            for kc in range(FC):
                if kc > 0:
                    ph.wt("scalar", "sP1", p["mm"][kc - 1])
                elif g == 1:
                    ph.wt("scalar", "sP1", pl[0]["mm"][FC - 1])
                ph.op("scalar", lambda sc, x=xslc(kc, g): sc.activation(
                    sc32[:, SC_XSQ:SC_XSQ + 512], x, AF.Square), att=[("sS", 1)])
            ph.wt("scalar", "sV", p["rows_a"])
            ph.op("scalar", lambda sc: sc.activation(
                row(R_SD), row(R_VAR), AF.Sqrt, bias=epsb[0:1, 0:1]), att=[("sS", 1)])

        for g in range(2):
            p = pl[g]
            if g == 1:
                ph.wt("tensor", "sV", pl[0]["rows_a"])
            for kc in range(FC):
                ph.add("tensor", lambda t, x=xslc(kc, g), k=kc: t.matmul(
                    psS1, ones32[:], x, start=(k == 0), stop=(k == FC - 1),
                    skip_group_check=True))
                ph.wt("tensor", "sS", p["sq"][kc])
                ph.op("tensor", lambda t, k=kc: t.matmul(
                    psS2, ones32[:], sc32[:, SC_XSQ:SC_XSQ + 512],
                    start=(k == 0), stop=(k == FC - 1), skip_group_check=True),
                    att=[("sP1", 1)])
            ph.wt("tensor", "sV", p["rows_c"])
            ph.add("tensor", lambda t: t.matmul(
                psA[:, 0:512], onesr[:], row(R_MEAN), start=True, stop=True,
                skip_group_check=True))
            ph.op("tensor", lambda t: t.matmul(
                psA[:, 512:1024], onesr[:], row(R_RSTD), start=True, stop=True,
                skip_group_check=True), att=[("sP1", 1)])
            if with_router:
                for kc in range(FC):
                    ph.wt("tensor", "sV", p["hn"][kc])
                    for ts_ in range(4):
                        hh = hslc(kc, g, 128, ts_ * 128)
                        hl = scbf[:, (kc % 2) * 512 + ts_ * 128:
                                  (kc % 2) * 512 + ts_ * 128 + 128]
                        pq = (psAV1 if g == 0 else psAV2)[:, ts_ * 128:
                                                          ts_ * 128 + 128]
                        ph.add("tensor", lambda t, pq=pq, hh=hh, k=kc,
                               s0=(kc == 0 and ts_ == 0): t.matmul(
                            pq, hh, wrt_hl[:, k * 128: k * 128 + 128],
                            start=s0, stop=False, skip_group_check=True))
                        ph.add("tensor", lambda t, pq=pq, hh=hh, k=kc: t.matmul(
                            pq, hh, wrt_hl[:, 2048 + k * 128: 2048 + k * 128 + 128],
                            start=False, stop=False, skip_group_check=True))
                        att = [("sP1", 1)] if ts_ == 3 else ()
                        ph.op("tensor", lambda t, pq=pq, hl=hl, k=kc: t.matmul(
                            pq, hl, wrt_hl[:, k * 128: k * 128 + 128],
                            start=False, stop=(k == FC - 1),
                            skip_group_check=True), att=att)
        if with_router:
            ph.op("tensor", lambda t: t.nop(), att=[("sP1", 1)])

        for g in range(2):
            p = pl[g]
            ph.wt("vector", "sP1", p["mm"][FC - 1])
            ph.add("vector", lambda v: v.tensor_scalar(
                row(R_MEAN), psS1, 1.0 / 2048.0, None, ALU.mult))
            ph.add("vector", lambda v: v.tensor_mul(
                row(R_T1), row(R_MEAN), row(R_MEAN)))
            ph.op("vector", lambda v: v.scalar_tensor_tensor(
                row(R_VAR), psS2, 1.0 / 2048.0, row(R_T1), ALU.mult,
                ALU.subtract), att=[("sV", 1)])
            ph.wt("vector", "sS", p["sqrt"])
            ph.op("vector", lambda v: v.reciprocal(row(R_RSTD), row(R_SD)),
                  att=[("sV", 1)])
            ph.wt("vector", "sP1", p["bcast"])
            for kc in range(FC):
                tt = sc32[:, SC_T:SC_T + 512]
                if with_router:
                    h32 = sc32[:, SC_H32:SC_H32 + 512]
                    if kc >= 2:
                        ph.wt("vector", "sP1", p["rmm"][kc - 2])
                    ph.add("vector", lambda v, x=xslc(kc, g): v.tensor_tensor(
                        tt, x, psA[:, 0:512], ALU.subtract))
                    ph.add("vector", lambda v: v.tensor_tensor(
                        h32, tt, psA[:, 512:1024], ALU.mult))
                    ph.add("vector", lambda v, h=hslc(kc, g):
                           v.tensor_copy(h, h32))
                    ph.op("vector", lambda v, h=hslc(kc, g), k=kc:
                          v.tensor_tensor(
                              scbf[:, (k % 2) * 512:(k % 2) * 512 + 512],
                              h32, h, ALU.subtract), att=[("sV", 1)])
                else:
                    ph.add("vector", lambda v, x=xslc(kc, g): v.tensor_tensor(
                        tt, x, psA[:, 0:512], ALU.subtract))
                    ph.op("vector", lambda v, h=hslc(kc, g): v.tensor_tensor(
                        h, tt, psA[:, 512:1024], ALU.mult), att=[("sV", 1)])
        return pl

    def emit_qkv(tag, wname, rope, bias, pl, ccsem):
        ph = Ph("qkv_" + tag)
        m_order = list(range(16, 32)) + list(range(32, 48)) + list(range(0, 16))
        recs = sched_strips(ph, [(wname, mi, FC) for mi in m_order])

        steps = []
        for j, mi in enumerate(m_order):
            for g in range(2):
                kind = "k" if 16 <= mi < 32 else ("v" if mi >= 32 else "q")
                steps.append(dict(mi=mi, g=g, j=j, rec=recs[j], kind=kind,
                                  gen=(2 * j + g) % 4, t=len(steps)))
        all_tp_vals = []
        for st in steps:
            st["mm_done"] = inc("sP1")
            if st["g"] == 1:
                st["wf"] = inc("sWf")
            if st["kind"] == "v" and st["g"] == 1:
                st["tp"] = [inc("sP1") for _ in range(8)]
        for st in steps:
            if st["kind"] == "v":
                st["ev"] = ("sES", inc("sES"))
            elif rope:
                st["plain"] = inc("sS")
        for st in steps:
            if st["kind"] == "v":
                if st["g"] == 1:
                    st["tpe"] = [inc("sV") for _ in range(8)]
            else:
                if rope:
                    st["v3"] = [inc("sV") for _ in range(3)]
                st["ev"] = ("sV", inc("sV"))
        for st in steps:
            if st["kind"] == "k":
                st["kv_sH"] = inc("sH", 16)
            elif st["kind"] == "v" and st["g"] == 1:
                st["vt_sH"] = inc("sH", 16)
        kv_last = C["sH"]
        cc_done = inc(ccsem)

        tp_seen = []
        ph.wt("tensor", "sV", pl[1]["hn"][FC - 1])
        for st in steps:
            if st["g"] == 0:
                ph.wt("tensor", "sW", st["rec"]["sW_after"])
            if st["t"] >= 4:
                pv = steps[st["t"] - 4]
                ph.wt("tensor", pv["ev"][0], pv["ev"][1])
            gen = GEN[st["gen"]]
            for kc in range(FC):
                last = (kc == FC - 1)
                att = ([("sP1", 1)] + ([("sWf", 1)] if st["g"] == 1 else [])) \
                    if last else ()
                ph.op("tensor", lambda t, gen=gen, w=wslice(st["rec"], kc),
                      h=hslc(kc, st["g"]), k=kc: t.matmul(
                          gen, w, h, start=(k == 0), stop=(k == FC - 1),
                          skip_group_check=True), att=att,
                      selfwait=("sP1", st["mm_done"]) if last else None)
            if st["kind"] == "v" and st["g"] == 1:
                ph.wt("tensor", "sES", st["ev"][1])
                for c in range(8):
                    gidx = len(tp_seen)
                    if gidx >= 2:
                        ph.wt("tensor", "sV", tp_seen[gidx - 2])
                    ph.op("tensor", lambda t, c=c: t.transpose(
                        psT[c % 2],
                        u2[:, U2_VS + c * 128: U2_VS + c * 128 + 128],
                        cbi[:]), att=[("sP1", 1)])
                    tp_seen.append(st["tpe"][c])

        prev_v = [None]
        last_rt = {0: None, 1: None}
        for st in steps:
            if st["kind"] == "v":
                ph.wt("scalar", "sP1", st["mm_done"])
                if prev_v[0] is not None and st["g"] == 0:
                    ph.wt("scalar", "sP1", prev_v[0]["tp"][7])
                dst = u2[:, U2_VS + st["g"] * 512: U2_VS + st["g"] * 512 + 512]
                if bias:
                    ph.op("scalar", lambda sc, d=dst, gen=GEN[st["gen"]],
                          b=cf[:, 128 + st["mi"]:129 + st["mi"]]:
                          sc.activation(d, gen, AF.Identity, bias=b),
                          att=[("sES", 1)])
                else:
                    ph.op("scalar", lambda sc, d=dst, gen=GEN[st["gen"]]:
                          sc.activation(d, gen, AF.Identity), att=[("sES", 1)])
                if st["g"] == 1:
                    prev_v[0] = st
            elif rope:
                ph.wt("scalar", "sP1", st["mm_done"])
                sl = st["t"] % 2
                if last_rt[sl] is not None:
                    ph.wt("scalar", last_rt[sl]["ev"][0], last_rt[sl]["ev"][1])
                dst = u2[:, U2_RT + sl * 512: U2_RT + sl * 512 + 512]
                ph.op("scalar", lambda sc, d=dst, gen=GEN[st["gen"]]:
                      sc.activation(d, gen, AF.Identity), att=[("sS", 1)])
                last_rt[sl] = st

        tb = scbf[:, 0:512]
        for st in steps:
            t_i = st["t"]
            if st["kind"] == "v":
                if st["g"] == 1:
                    vt = U2_VT0 if (st["j"] % 2 == 0) else U2_VT1
                    pv2 = [s2 for s2 in steps
                           if s2["kind"] == "v" and s2["g"] == 1
                           and s2["j"] == st["j"] - 2]
                    for c in range(8):
                        ph.wt("vector", "sP1", st["tp"][c])
                        if c == 0 and pv2:
                            ph.wt("vector", "sH", pv2[0]["vt_sH"])
                        ph.op("vector", lambda v, c=c, vt=vt: v.tensor_copy(
                            u2[:, vt + c * 128: vt + c * 128 + 128], psT[c % 2]),
                            att=[("sV", 1)])
                continue
            if rope:
                ph.wt("vector", "sS", st["plain"])
                shuf = u2[:, U2_RT + 1024 + (t_i % 2) * 512:
                          U2_RT + 1024 + (t_i % 2) * 512 + 512]
                plain = u2[:, U2_RT + (t_i % 2) * 512:
                           U2_RT + (t_i % 2) * 512 + 512]
                ph.op("vector", lambda v, s2=shuf, p_=plain:
                      v.stream_shuffle(s2, p_, [p ^ 1 for p in range(32)]),
                      att=[("sV", 1)])
                cs = cosb[:, st["g"] * GW: st["g"] * GW + 512]
                sn = sinb[:, st["g"] * GW: st["g"] * GW + 512]
                ph.op("vector", lambda v, p_=plain, cs=cs: v.tensor_tensor(
                    tb, p_, cs, ALU.mult), att=[("sV", 1)])
                ph.wt("vector", "sV", st["v3"][0])
                ph.op("vector", lambda v, s2=shuf, sn=sn: v.tensor_tensor(
                    s2, s2, sn, ALU.mult), att=[("sV", 1)])
                ph.wt("vector", "sV", st["v3"][2])
            if st["kind"] == "k":
                dst = u2[:, U2_KV + (t_i % 2) * 512: U2_KV + (t_i % 2) * 512 + 512]
                if t_i >= 2 and steps[t_i - 2]["kind"] == "k":
                    ph.wt("vector", "sH", steps[t_i - 2]["kv_sH"])
            else:
                dst = u1[:, U1_Q + st["mi"] * TOK + st["g"] * GW:
                         U1_Q + st["mi"] * TOK + st["g"] * GW + 512]
            if rope:
                ph.op("vector", lambda v, d=dst,
                      s2=u2[:, U2_RT + 1024 + (t_i % 2) * 512:
                            U2_RT + 1024 + (t_i % 2) * 512 + 512]:
                      v.tensor_add(d, tb, s2), att=[("sV", 1)])
            else:
                ph.wt("vector", "sP1", st["mm_done"])
                if bias:
                    ph.op("vector", lambda v, d=dst, gen=GEN[st["gen"]],
                          b=cf[:, 128 + st["mi"]:129 + st["mi"]]:
                          v.tensor_scalar(d, gen, b, None, ALU.add),
                          att=[("sV", 1)])
                else:
                    ph.op("vector", lambda v, d=dst, gen=GEN[st["gen"]]:
                          v.tensor_copy(d, gen), att=[("sV", 1)])

        for st in steps:
            if st["kind"] == "k":
                ph.wt("sync", st["ev"][0], st["ev"][1])
                ph.op("sync", lambda s, st=st: s.dma_start(
                    out=cc_in[(st["mi"] - 16) * 128:(st["mi"] - 16) * 128 + 128,
                              st["g"] * GW: st["g"] * GW + 512],
                    in_=u2[:, U2_KV + (st["t"] % 2) * 512:
                           U2_KV + (st["t"] % 2) * 512 + 512]),
                    att=[("sH", 16)])
            elif st["kind"] == "v" and st["g"] == 1:
                ph.wt("sync", "sV", st["tpe"][7])
                h = st["mi"] - 32
                hf = h // 8
                colv = (h * 128) % 1024
                base = (2048 + hf) * TOK + colv
                vt = U2_VT0 if (st["j"] % 2 == 0) else U2_VT1
                ph.op("sync", lambda s, base=base, vt=vt: s.dma_start(
                    out=bass.AP(cc_in.tensor, base,
                                [[2 * TOK, 128], [256 * TOK, 8], [1, 128]]),
                    in_=u2[:, vt:vt + 1024].rearrange("p (c f) -> p c f", f=128)),
                    att=[("sH", 16)])

        ph.wt("gpsimd", "sH", kv_last)
        ph.op("gpsimd", lambda g: g.collective_compute(
            "AllGather", ALU.bypass,
            replica_groups=[[0, 1, 2, 3], [4, 5, 6, 7]],
            ins=[cc_in.opt()], outs=[cc_out.opt()]), att=[(ccsem, 1)])

        qev = {}
        for st in steps:
            if st["kind"] == "q":
                qev[(st["g"], st["mi"])] = st["ev"][1]
        return dict(cc_done=cc_done, steps=steps, qev=qev, ccsem=ccsem)

    def emit_attention(tag, qkv_info):
        ph = Ph("att_" + tag)
        cc_done = qkv_info["cc_done"]

        segs = []
        steps = []
        for g in range(2):
            for hp in range(8):
                for r in range(4):
                    for sh in range(g + 1):
                        seg = dict(g=g, hp=hp, r=r, sh=sh, gseg=len(segs))
                        segs.append(seg)
                        for m in range(4):
                            c = sh * 4 + m
                            steps.append(dict(
                                g=g, hp=hp, r=r, c=c, seg=seg, i=len(steps),
                                partial=(c >= 4 * g), mloc=c - 4 * g))
        byp = {}
        for i, st in enumerate(steps):
            byp.setdefault((st["g"], st["hp"]), []).append(i)
        for kk, idxs in byp.items():
            steps[idxs[0]]["pstart"] = True
            steps[idxs[-1]]["pstop"] = True
        for st in steps:
            st.setdefault("pstart", False)
            st.setdefault("pstop", False)
        for sg in segs:
            sg["last_step"] = max(st["i"] for st in steps if st["seg"] is sg)

        n = len(steps)
        for sg in segs:
            sg["ld"] = [inc("sH", 16) for _ in range(3)]
        pe_order = []
        for i in range(n):
            pe_order.append(("sc", i))
            if i >= 2:
                pe_order.append(("av", i - 2))
                if steps[i - 2]["pstop"]:
                    pe_order.append(("ep", (steps[i - 2]["g"], steps[i - 2]["hp"])))
        for i in (n - 2, n - 1):
            pe_order.append(("av", i))
            if steps[i]["pstop"]:
                pe_order.append(("ep", (steps[i]["g"], steps[i]["hp"])))
        epil = {}
        for kind, a in pe_order:
            if kind == "sc":
                steps[a]["sc_done"] = inc("sP1")
            elif kind == "av":
                steps[a]["av_done"] = inc("sP2")
            else:
                epil.setdefault(a, {})["bcast"] = inc("sP1")
        for st in steps:
            st["exp_done"] = inc("sS")
        for st in steps:
            if st["partial"]:
                st["mask_done"] = inc("sV")
            if st["pstop"]:
                ep = epil[(st["g"], st["hp"])]
                ep["recip"] = inc("sV")
                ep["fin"] = inc("sV")

        def prev_pair(pk):
            g0, hp0 = pk
            if hp0 > 0:
                return (g0, hp0 - 1)
            return (g0 - 1, 7) if g0 > 0 else None

        ph.wt("sync", qkv_info["ccsem"], cc_done)
        for sg in segs:
            gseg = sg["gseg"]
            if gseg >= 2:
                ph.wt("sync", "sP2",
                      steps[segs[gseg - 2]["last_step"]]["av_done"])
            par = gseg % 2
            h1 = sg["hp"] * 2
            r = sg["r"]
            colr = sg["sh"] * 512
            for hh, koff in ((h1, 0), (h1 + 1, 512)):
                ph.op("sync", lambda s, hh=hh, koff=koff, r=r, colr=colr,
                      par=par: s.dma_start(
                          out=u2[:, U2_K + par * 1024 + koff:
                                 U2_K + par * 1024 + koff + 512],
                          in_=cc_out[r * 4096 + hh * 128:
                                     r * 4096 + hh * 128 + 128,
                                     colr: colr + 512]), att=[("sH", 16)])
            hp = sg["hp"]
            hf = (hp * 256) // 1024
            colv = (hp * 256) % 1024
            base = (r * 4096 + 2048 + hf) * TOK + colv + 2 * colr * TOK
            ph.op("sync", lambda s, base=base, par=par: s.dma_start(
                out=u2[:, U2_V + par * 1024: U2_V + par * 1024 + 1024]
                .rearrange("p (c f) -> p c f", f=256),
                in_=bass.AP(cc_out.tensor, base,
                            [[2 * TOK, 128], [256 * TOK, 4], [1, 256]])),
                att=[("sH", 16)])

        def emit_sc(i):
            st = steps[i]
            sg = st["seg"]
            par = i % 2
            pss = psA if par == 0 else psB
            if st["pstart"]:
                ph.wt("tensor", "sV",
                      qkv_info["qev"][(st["g"], st["hp"] * 2 + 1)])
            ph.wt("tensor", "sH", sg["ld"][1])
            if i >= 2:
                ph.wt("tensor", "sS", steps[i - 2]["exp_done"])
            kt = st["c"] % 4
            kpar = sg["gseg"] % 2
            h1 = st["hp"] * 2
            q1 = u1[:, h1 * TOK + st["g"] * GW: h1 * TOK + st["g"] * GW + 512]
            q2 = u1[:, (h1 + 1) * TOK + st["g"] * GW:
                    (h1 + 1) * TOK + st["g"] * GW + 512]
            k1 = u2[:, U2_K + kpar * 1024 + kt * 128:
                    U2_K + kpar * 1024 + kt * 128 + 128]
            k2 = u2[:, U2_K + kpar * 1024 + 512 + kt * 128:
                    U2_K + kpar * 1024 + 512 + kt * 128 + 128]
            ph.add("tensor", lambda t, pss=pss, k1=k1, q1=q1: t.matmul(
                pss[:, 0:512], k1, q1, start=True, stop=True,
                skip_group_check=True))
            ph.op("tensor", lambda t, pss=pss, k2=k2, q2=q2: t.matmul(
                pss[:, 512:1024], k2, q2, start=True, stop=True,
                skip_group_check=True), att=[("sP1", 1)])

        def emit_av(i):
            st = steps[i]
            sg = st["seg"]
            par = i % 2
            ph.wt("tensor", "sS", st["exp_done"])
            ph.wt("tensor", "sH", sg["ld"][2])
            if st["pstart"]:
                pk2 = prev_pair((st["g"], st["hp"]))
                if pk2 is not None and pk2 in epil:
                    ph.wt("tensor", "sV", epil[pk2]["fin"])
            kt = st["c"] % 4
            kpar = sg["gseg"] % 2
            ex = u2[:, U2_EXP + par * 1024: U2_EXP + par * 1024 + 1024]
            vb = u2[:, U2_V + kpar * 1024: U2_V + kpar * 1024 + 1024]
            ph.add("tensor", lambda t, v1=vb[:, kt * 256: kt * 256 + 128],
                   e1=ex[:, 0:512], st=st: t.matmul(
                       psAV1[:], v1, e1, start=st["pstart"], stop=st["pstop"],
                       skip_group_check=True))
            ph.add("tensor", lambda t, v2=vb[:, kt * 256 + 128: kt * 256 + 256],
                   e2=ex[:, 512:1024], st=st: t.matmul(
                       psAV2[:], v2, e2, start=st["pstart"], stop=st["pstop"],
                       skip_group_check=True))
            ph.add("tensor", lambda t, e1=ex[:, 0:512], st=st: t.matmul(
                psS1, onesb16[:], e1, start=st["pstart"], stop=st["pstop"],
                skip_group_check=True))
            ph.op("tensor", lambda t, e2=ex[:, 512:1024], st=st: t.matmul(
                psS2, onesb16[:], e2, start=st["pstart"], stop=st["pstop"],
                skip_group_check=True), att=[("sP2", 1)])

        def emit_ep(pk):
            ep = epil[pk]
            ph.wt("tensor", "sV", ep["recip"])
            ph.add("tensor", lambda t: t.matmul(
                psX[:, 0:512], onesr[:], rows[0:1, 0:512], start=True,
                stop=True, skip_group_check=True))
            ph.op("tensor", lambda t: t.matmul(
                psX[:, 512:1024], onesr[:], rows[0:1, 512:1024], start=True,
                stop=True, skip_group_check=True), att=[("sP1", 1)])

        for kind, a in pe_order:
            (emit_sc if kind == "sc" else emit_av if kind == "av"
             else emit_ep)(a)

        for i, st in enumerate(steps):
            par = i % 2
            pss = psA if par == 0 else psB
            if st["partial"]:
                ph.wt("scalar", "sV", st["mask_done"])
            else:
                ph.wt("scalar", "sP1", st["sc_done"])
            if i >= 2:
                ph.wt("scalar", "sP2", steps[i - 2]["av_done"])
            ph.op("scalar", lambda sc, pss=pss, par=par: sc.activation(
                u2[:, U2_EXP + par * 1024: U2_EXP + par * 1024 + 1024],
                pss[:, 0:1024], AF.Exp, scale=SCALE), att=[("sS", 1)])

        rc1 = sc32[:, SC_PROB:SC_PROB + 512]
        rc2 = sc32[:, SC_PROB + 512:SC_PROB + 1024]
        for i, st in enumerate(steps):
            par = i % 2
            pss = psA if par == 0 else psB
            if st["partial"]:
                ph.wt("vector", "sP1", st["sc_done"])
                m = st["mloc"]
                wm = 128 * (m + 1)
                r = st["seg"]["r"]
                mask = u2[:, U2_MASK + r * MW + 385 - 128 * m:
                          U2_MASK + r * MW + 385 - 128 * m + wm]
                pv = pss[:, 0:1024].rearrange("p (h w) -> p h w", w=512)
                ph.op("vector", lambda v, pv=pv, mask=mask, wm=wm:
                      v.tensor_tensor(
                          pv[:, :, 0:wm], pv[:, :, 0:wm],
                          mask.unsqueeze(1).broadcast_to([128, 2, wm]),
                          ALU.add), att=[("sV", 1)])
            if st["pstop"]:
                ep = epil[(st["g"], st["hp"])]
                ph.wt("vector", "sP2", st["av_done"])
                ph.add("vector", lambda v: v.reciprocal(rows[0:1, 0:512], psS1))
                ph.op("vector", lambda v: v.reciprocal(rows[0:1, 512:1024], psS2),
                      att=[("sV", 1)])
                ph.wt("vector", "sP1", ep["bcast"])
                ph.add("vector", lambda v: v.tensor_copy(rc1, psX[:, 0:512]))
                ph.add("vector", lambda v: v.tensor_copy(rc2, psX[:, 512:1024]))
                h1 = st["hp"] * 2
                d1 = hb[:, h1 * TOK + st["g"] * GW: h1 * TOK + st["g"] * GW + 512]
                d2 = hb[:, (h1 + 1) * TOK + st["g"] * GW:
                        (h1 + 1) * TOK + st["g"] * GW + 512]
                ph.add("vector", lambda v, d1=d1: v.tensor_tensor(
                    d1, psAV1[:], rc1, ALU.mult))
                ph.op("vector", lambda v, d2=d2: v.tensor_tensor(
                    d2, psAV2[:], rc2, ALU.mult), att=[("sV", 1)])
        return {0: epil[(0, 7)]["fin"], 1: epil[(1, 7)]["fin"]}

    def emit_outproj(tag, wname, bias, fins):
        ph = Ph("out_" + tag)
        recs = sched_strips(ph, [(wname, mi, FC) for mi in range(16)])
        steps = []
        for j in range(16):
            for g in range(2):
                steps.append(dict(mi=j, g=g, rec=recs[j], gen=len(steps) % 4))
        for st in steps:
            st["mm"] = inc("sP1")
            if st["g"] == 1:
                st["wf"] = inc("sWf")
        for st in steps:
            st["ev"] = inc("sV")
        fin_waited = [False, False]
        for i, st in enumerate(steps):
            if not fin_waited[st["g"]]:
                ph.wt("tensor", "sV", fins[st["g"]])
                fin_waited[st["g"]] = True
            if st["g"] == 0:
                ph.wt("tensor", "sW", st["rec"]["sW_after"])
            if i >= 4:
                ph.wt("tensor", "sV", steps[i - 4]["ev"])
            gen = GEN[st["gen"]]
            for kc in range(FC):
                last = (kc == FC - 1)
                att = ([("sP1", 1)] + ([("sWf", 1)] if st["g"] == 1 else [])) \
                    if last else ()
                ph.op("tensor", lambda t, gen=gen, w=wslice(st["rec"], kc),
                      h=hslc(kc, st["g"]), k=kc: t.matmul(
                          gen, w, h, start=(k == 0), stop=(k == FC - 1),
                          skip_group_check=True), att=att,
                      selfwait=("sP1", st["mm"]) if last else None)
        for st in steps:
            ph.wt("vector", "sP1", st["mm"])
            xs = xb[:, st["mi"] * TOK + st["g"] * GW:
                    st["mi"] * TOK + st["g"] * GW + 512]
            if bias:
                ph.op("vector", lambda v, xs=xs, gen=GEN[st["gen"]],
                      b=cf[:, 176 + st["mi"]:177 + st["mi"]]:
                      v.scalar_tensor_tensor(xs, gen, b, xs, ALU.add, ALU.add),
                      att=[("sV", 1)])
            else:
                ph.op("vector", lambda v, xs=xs, gen=GEN[st["gen"]]:
                      v.tensor_add(xs, xs, gen), att=[("sV", 1)])

    def emit_router_ep(tag, router_done):
        ph = Ph("rt_" + tag)
        prob = sc32[:, SC_PROB:SC_PROB + 1024]
        wcur = sc32[:, SC_WFT:SC_WFT + 1024]
        tmp = sc32[:, 0:1024]
        wfts = scbf[:, 0:1024]
        mxb = scbf[:, 1024:2048]
        negm = sc32[:, 3584:3592]
        rsum = sc32[:, 3592:3600]
        rcp = sc32[:, 3600:3608]
        thr = sc32[:, 3608:3616]
        P3 = lambda a_: a_.rearrange("p (a e) -> p a e", e=128)
        bc8 = lambda a_: a_.unsqueeze(2).broadcast_to([128, 8, 128])

        v_ms = inc("sV")
        v_mx = [inc("sV") for _ in range(8)]
        v_evac = inc("sV")
        v_negm = inc("sV")
        v_sub = inc("sV")
        v_sum = inc("sV")
        v_rcp = inc("sV")
        v_thr = [inc("sV") for _ in range(4)]
        v_ssum = inc("sV")
        v_rcp2 = inc("sV")
        v_wf = inc("sV")
        v_wft = [inc("sV") for _ in range(8)]
        v_ck = [inc("sV") for _ in range(16)]
        s_exp = inc("sS")
        t_tp = [inc("sP1") for _ in range(8)]
        t_ck = [inc("sP1") for _ in range(16)]

        ph.op("vector", lambda v: v.memset(mxb, 1.0), att=[("sV", 1)])
        ph.wt("vector", "sV", v_ms)
        for k in range(8):
            ph.op("vector", lambda v, k=k: v.tensor_scalar(
                mxb[:, k * 128: k * 128 + 128],
                mxb[:, k * 128: k * 128 + 128],
                mixb[:, k:k + 1], None, ALU.mult), att=[("sV", 1)])
        ph.wt("vector", "sV", v_mx[-1])

        ph.wt("vector", "sP1", router_done)
        ph.add("vector", lambda v: v.tensor_copy(prob[:, 0:512], psAV1[:]))
        ph.op("vector", lambda v: v.tensor_copy(prob[:, 512:1024], psAV2[:]),
              att=[("sV", 1)])
        ph.op("vector", lambda v: v.reduce_max(
            negm.unsqueeze(2), P3(prob), axis=AX.X, negate=True),
            att=[("sV", 1)])
        ph.wt("vector", "sV", v_negm)
        ph.op("vector", lambda v: v.tensor_tensor(
            P3(prob), P3(prob), bc8(negm), ALU.add), att=[("sV", 1)])
        ph.wt("vector", "sS", s_exp)
        ph.op("vector", lambda v: v.reduce_sum(
            rsum.unsqueeze(2), P3(prob), axis=AX.X), att=[("sV", 1)])
        ph.wt("vector", "sV", v_sum)
        ph.add("vector", lambda v: v.tensor_scalar(rsum, rsum, 1e-30, None,
                                                   ALU.add))
        ph.op("vector", lambda v: v.reciprocal(rcp, rsum), att=[("sV", 1)])
        ph.wt("vector", "sV", v_rcp)
        ph.add("vector", lambda v: v.tensor_tensor(
            P3(prob), P3(prob), bc8(rcp), ALU.mult))
        ph.add("vector", lambda v: v.tensor_copy(wcur, prob))
        for it in range(4):
            ph.op("vector", lambda v: v.reduce_max(
                thr.unsqueeze(2), P3(wcur), axis=AX.X), att=[("sV", 1)])
            ph.wt("vector", "sV", v_thr[it])
            ph.add("vector", lambda v: v.tensor_tensor(
                P3(tmp), P3(wcur), bc8(thr), ALU.is_ge))
            ph.add("vector", lambda v: v.scalar_tensor_tensor(
                wcur, tmp, -2.0, wcur, ALU.mult, ALU.add))
        ph.add("vector", lambda v: v.tensor_tensor(
            P3(tmp), P3(prob), bc8(thr), ALU.is_ge))
        ph.add("vector", lambda v: v.tensor_mul(wcur, prob, tmp))
        ph.op("vector", lambda v: v.reduce_sum(
            rsum.unsqueeze(2), P3(wcur), axis=AX.X), att=[("sV", 1)])
        ph.wt("vector", "sV", v_ssum)
        ph.add("vector", lambda v: v.tensor_scalar(rsum, rsum, 1e-8, None,
                                                   ALU.add))
        ph.op("vector", lambda v: v.reciprocal(rcp, rsum), att=[("sV", 1)])
        ph.wt("vector", "sV", v_rcp2)
        ph.op("vector", lambda v: v.tensor_tensor(
            P3(wcur), P3(wcur), bc8(rcp), ALU.mult), att=[("sV", 1)])
        for tt in range(8):
            ph.wt("vector", "sP1", t_tp[tt])
            ph.op("vector", lambda v, tt=tt: v.tensor_scalar(
                wfts[:, tt * 128: tt * 128 + 128],
                psB[:, (tt % 2) * 512:(tt % 2) * 512 + 128],
                cf[:, 192:193], None, ALU.mult), att=[("sV", 1)])
        for j in range(16):
            ph.wt("vector", "sP1", t_ck[j])
            k, g = j // 2, j % 2
            ph.op("vector", lambda v, k=k, g=g, j=j: v.tensor_copy(
                u1[:, U1_CK + (2 * k + g) * 512:
                   U1_CK + (2 * k + g) * 512 + 512],
                psA[:, (j % 2) * 512:(j % 2) * 512 + 512]), att=[("sV", 1)])

        ph.wt("scalar", "sV", v_sub)
        ph.op("scalar", lambda sc: sc.activation(prob, prob, AF.Exp),
              att=[("sS", 1)])

        for tt in range(8):
            ph.wt("tensor", "sV", v_wf)
            if tt >= 2:
                ph.wt("tensor", "sV", v_wft[tt - 2])
            ph.op("tensor", lambda t, tt=tt: t.transpose(
                psB[:, (tt % 2) * 512:(tt % 2) * 512 + 128],
                sc32[:, SC_WFT + tt * 128: SC_WFT + tt * 128 + 128],
                cf[:, 0:128]), att=[("sP1", 1)])
        for j in range(16):
            k, g = j // 2, j % 2
            ph.wt("tensor", "sV", v_wft[7])
            if j >= 2:
                ph.wt("tensor", "sV", v_ck[j - 2])
            ph.op("tensor", lambda t, k=k, g=g, j=j: t.matmul(
                psA[:, (j % 2) * 512:(j % 2) * 512 + 512],
                mxb[:, k * 128: k * 128 + 128],
                wfts[:, g * 512: g * 512 + 512], start=True, stop=True,
                skip_group_check=True), att=[("sP1", 1)])
        return v_ck[-1]

    def emit_gateup(tag, ck_done):
        ph = Ph("gu_" + tag)
        order = []
        for k in range(8):
            for c2 in range(8):
                order.append(("wg", k * 8 + c2, FC))
                order.append(("wu", k * 8 + c2, FC))
        recs = sched_strips(ph, order)
        steps = []
        for k in range(8):
            for c2 in range(8):
                for g in range(2):
                    steps.append(dict(k=k, c=c2, g=g,
                                      rg=recs[(k * 8 + c2) * 2],
                                      ru=recs[(k * 8 + c2) * 2 + 1],
                                      par=len(steps) % 2))
        for st in steps:
            st["mm"] = inc("sP1")
            if st["g"] == 1:
                inc("sWf", 2)
        for st in steps:
            st["silu"] = inc("sS")
        for st in steps:
            st["ev"] = inc("sV")
        ph.wt("tensor", "sV", ck_done)
        for i, st in enumerate(steps):
            if st["g"] == 0:
                ph.wt("tensor", "sW", st["ru"]["sW_after"])
            if i >= 2:
                ph.wt("tensor", "sV", steps[i - 2]["ev"])
            pg = GEN[2 * st["par"]]
            pu = GEN[2 * st["par"] + 1]
            for kc in range(FC):
                ph.add("tensor", lambda t, pg=pg, w=wslice(st["rg"], kc),
                       h=hslc(kc, st["g"]), k=kc: t.matmul(
                           pg, w, h, start=(k == 0), stop=(k == FC - 1),
                           skip_group_check=True))
                last = (kc == FC - 1)
                att = ([("sP1", 1)] + ([("sWf", 2)] if st["g"] == 1 else [])) \
                    if last else ()
                ph.op("tensor", lambda t, pu=pu, w=wslice(st["ru"], kc),
                      h=hslc(kc, st["g"]), k=kc: t.matmul(
                          pu, w, h, start=(k == 0), stop=(k == FC - 1),
                          skip_group_check=True), att=att,
                      selfwait=("sP1", st["mm"]) if last else None)
        for i, st in enumerate(steps):
            ph.wt("scalar", "sP1", st["mm"])
            if i >= 2:
                ph.wt("scalar", "sV", steps[i - 2]["ev"])
            sg = scbf[:, st["par"] * 1024: st["par"] * 1024 + 512]
            ph.op("scalar", lambda sc, sg=sg, pg=GEN[2 * st["par"]]:
                  sc.activation(sg, pg, AF.Silu), att=[("sS", 1)])
        for i, st in enumerate(steps):
            ph.wt("vector", "sP1", st["mm"])
            sg = scbf[:, st["par"] * 1024: st["par"] * 1024 + 512]
            ub = scbf[:, st["par"] * 1024 + 512: st["par"] * 1024 + 1024]
            ckt = u1[:, U1_CK + (2 * st["k"] + st["g"]) * 512:
                     U1_CK + (2 * st["k"] + st["g"]) * 512 + 512]
            ph.add("vector", lambda v, ub=ub, pu=GEN[2 * st["par"] + 1],
                   ckt=ckt: v.tensor_tensor(ub, pu, ckt, ALU.mult))
            ph.wt("vector", "sS", st["silu"])
            acc = u1[:, U1_ACC + st["c"] * TOK + st["g"] * GW:
                     U1_ACC + st["c"] * TOK + st["g"] * GW + 512]
            if st["k"] == 0:
                ph.op("vector", lambda v, acc=acc, sg=sg, ub=ub:
                      v.tensor_mul(acc, sg, ub), att=[("sV", 1)])
            else:
                ph.add("vector", lambda v, sg=sg, ub=ub:
                       v.tensor_mul(ub, sg, ub))
                ph.op("vector", lambda v, acc=acc, ub=ub:
                      v.tensor_add(acc, acc, ub), att=[("sV", 1)])
        return {g: max(st["ev"] for st in steps if st["g"] == g)
                for g in range(2)}

    def emit_shared(tag):
        ph = Ph("sh_" + tag)
        order = []
        for mi in range(16):
            order.append(("wsg", mi, FC))
            order.append(("wsu", mi, FC))
        recs = sched_strips(ph, order)
        steps = []
        for mi in range(16):
            for g in range(2):
                steps.append(dict(mi=mi, g=g, rg=recs[mi * 2],
                                  ru=recs[mi * 2 + 1], par=len(steps) % 2))
        for st in steps:
            st["mm"] = inc("sP1")
            if st["g"] == 1:
                inc("sWf", 2)
        for st in steps:
            st["silu"] = inc("sS")
        for st in steps:
            st["ev"] = inc("sV")
        for i, st in enumerate(steps):
            if st["g"] == 0:
                ph.wt("tensor", "sW", st["ru"]["sW_after"])
            if i >= 2:
                ph.wt("tensor", "sV", steps[i - 2]["ev"])
            pg = GEN[2 * st["par"]]
            pu = GEN[2 * st["par"] + 1]
            for kc in range(FC):
                ph.add("tensor", lambda t, pg=pg, w=wslice(st["rg"], kc),
                       h=hslc(kc, st["g"]), k=kc: t.matmul(
                           pg, w, h, start=(k == 0), stop=(k == FC - 1),
                           skip_group_check=True))
                last = (kc == FC - 1)
                att = ([("sP1", 1)] + ([("sWf", 2)] if st["g"] == 1 else [])) \
                    if last else ()
                ph.op("tensor", lambda t, pu=pu, w=wslice(st["ru"], kc),
                      h=hslc(kc, st["g"]), k=kc: t.matmul(
                          pu, w, h, start=(k == 0), stop=(k == FC - 1),
                          skip_group_check=True), att=att,
                      selfwait=("sP1", st["mm"]) if last else None)
        for i, st in enumerate(steps):
            ph.wt("scalar", "sP1", st["mm"])
            if i >= 2:
                ph.wt("scalar", "sV", steps[i - 2]["ev"])
            sg = scbf[:, st["par"] * 1024: st["par"] * 1024 + 512]
            ph.op("scalar", lambda sc, sg=sg, pg=GEN[2 * st["par"]]:
                  sc.activation(sg, pg, AF.Silu), att=[("sS", 1)])
        for i, st in enumerate(steps):
            ph.wt("vector", "sP1", st["mm"])
            sg = scbf[:, st["par"] * 1024: st["par"] * 1024 + 512]
            ub = scbf[:, st["par"] * 1024 + 512: st["par"] * 1024 + 1024]
            ph.add("vector", lambda v, ub=ub, pu=GEN[2 * st["par"] + 1]:
                   v.tensor_copy(ub, pu))
            ph.wt("vector", "sS", st["silu"])
            dst = u2[:, st["mi"] * TOK + st["g"] * GW:
                     st["mi"] * TOK + st["g"] * GW + 512]
            ph.op("vector", lambda v, dst=dst, sg=sg, ub=ub:
                  v.tensor_mul(dst, sg, ub), att=[("sV", 1)])
        return {g: max(st["ev"] for st in steps if st["g"] == g)
                for g in range(2)}

    def emit_down(tag, gu_last, sh_last):
        ph = Ph("dn_" + tag)
        order = []
        for mi in range(16):
            order.append(("wd", mi, 8))
            order.append(("wsd", mi, FC))
        recs = sched_strips(ph, order)
        steps = []
        for mi in range(16):
            for g in range(2):
                steps.append(dict(mi=mi, g=g, rd=recs[mi * 2],
                                  rs=recs[mi * 2 + 1], gen=len(steps) % 4))
        for st in steps:
            st["mm"] = inc("sP1")
            if st["g"] == 1:
                inc("sWf", 2)
        for st in steps:
            st["ev"] = inc("sV")
        src_waited = [False, False]
        for i, st in enumerate(steps):
            if not src_waited[st["g"]]:
                ph.wt("tensor", "sV",
                      max(gu_last[st["g"]], sh_last[st["g"]]))
                src_waited[st["g"]] = True
            if st["g"] == 0:
                ph.wt("tensor", "sW", st["rs"]["sW_after"])
            if i >= 4:
                ph.wt("tensor", "sV", steps[i - 4]["ev"])
            gen = GEN[st["gen"]]
            for kc in range(8):
                accs = u1[:, U1_ACC + kc * TOK + st["g"] * GW:
                          U1_ACC + kc * TOK + st["g"] * GW + 512]
                ph.add("tensor", lambda t, gen=gen, w=wslice(st["rd"], kc),
                       a=accs, k=kc: t.matmul(
                           gen, w, a, start=(k == 0), stop=False,
                           skip_group_check=True))
            for kc in range(FC):
                shs = u2[:, kc * TOK + st["g"] * GW:
                         kc * TOK + st["g"] * GW + 512]
                last = (kc == FC - 1)
                att = ([("sP1", 1)] + ([("sWf", 2)] if st["g"] == 1 else [])) \
                    if last else ()
                ph.op("tensor", lambda t, gen=gen, w=wslice(st["rs"], kc),
                      sh=shs, k=kc: t.matmul(
                          gen, w, sh, start=False, stop=(k == FC - 1),
                          skip_group_check=True), att=att,
                      selfwait=("sP1", st["mm"]) if last else None)
        for st in steps:
            ph.wt("vector", "sP1", st["mm"])
            xs = xb[:, st["mi"] * TOK + st["g"] * GW:
                    st["mi"] * TOK + st["g"] * GW + 512]
            ph.op("vector", lambda v, xs=xs, gen=GEN[st["gen"]]:
                  v.tensor_add(xs, xs, gen), att=[("sV", 1)])

    def emit_mask_reload():
        ph = Ph("mreload")
        ph.wt("sync", "sP1", C["sP1"])
        inc("sH", 16)
        ph.op("sync", lambda s: s.dma_start(
            out=u2[:, U2_MASK:U2_MASK + 4 * MW]
            .rearrange("p (r w) -> p r w", w=MW),
            in_=d_msk.rearrange("(r p) w -> p r w", p=128)), att=[("sH", 16)])
        ph.wt("sync", "sH", C["sH"])

    sub = 0
    if sub < N_SUB:
        pla = emit_ln("a", with_router=False)
        qa = emit_qkv("a", "wqa", rope=True, bias=False, pl=pla, ccsem="sC")
        if os.environ.get("ATH_STOP") == "qkv":
            ph = Ph("qkvbar")
            ph.wt("sync", "sC", 1)
            ph.wt("sync", "sV", C["sV"])
            ph.wt("sync", "sES", C["sES"])
        else:
            fa = emit_attention("a", qa)
            emit_outproj("a", "woa", bias=False, fins=fa)
        sub += 1
    STOP = os.environ.get("ATH_STOP", "")
    if sub < N_SUB:
        pl = emit_ln("f1", with_router=True)
        if STOP != "lnf1":
            ck1 = emit_router_ep("f1", pl["router_done"])
            if STOP != "rtf1":
                gu1 = emit_gateup("f1", ck1)
                if STOP != "guf1":
                    sh1 = emit_shared("f1")
                    if STOP != "shf1":
                        emit_down("f1", gu1, sh1)
        sub += 1
    if sub < N_SUB:
        emit_mask_reload()
        plb = emit_ln("b", with_router=False)
        qb = emit_qkv("b", "wqb", rope=False, bias=True, pl=plb, ccsem="sC2")
        fb = emit_attention("b", qb)
        emit_outproj("b", "wob", bias=True, fins=fb)
        sub += 1
    if sub < N_SUB:
        pl = emit_ln("f2", with_router=True)
        ck2 = emit_router_ep("f2", pl["router_done"])
        gu2 = emit_gateup("f2", ck2)
        sh2 = emit_shared("f2")
        emit_down("f2", gu2, sh2)
        sub += 1

    assert C["sWf"] == wf_n[0], (C["sWf"], wf_n[0])

    ph = Ph("store")
    ph.wt("sync", "sV", C["sV"])
    ph.wt("sync", "sS", C["sS"])
    inc("sH", 16)
    ph.op("sync", lambda s: s.dma_start(
        out=d_out.rearrange("(k p) t -> p k t", p=128),
        in_=xb[:].rearrange("p (k t) -> p k t", t=TOK)), att=[("sH", 16)])
    if os.environ.get("ATH_DUMP"):
        dmp = {
            "dbg_hb": (nc.dram_tensor("dbg_hb", [128, FC * TOK], BF16,
                                      kind="ExternalOutput").ap(), hb[:]),
            "dbg_u1": (nc.dram_tensor("dbg_u1", [128, FC * TOK], BF16,
                                      kind="ExternalOutput").ap(), u1[:]),
            "dbg_u2": (nc.dram_tensor("dbg_u2", [128, FC * TOK], BF16,
                                      kind="ExternalOutput").ap(), u2[:]),
            "dbg_sc": (nc.dram_tensor("dbg_sc", [128, 2048], F32,
                                      kind="ExternalOutput").ap(),
                       sc32[:, 1536:3584]),
            "dbg_cc": (nc.dram_tensor("dbg_cc", [16384, TOK], BF16,
                                      kind="ExternalOutput").ap(), cc_out[:]),
            "dbg_mx": (nc.dram_tensor("dbg_mx", [128, 1024], BF16,
                                      kind="ExternalOutput").ap(),
                       scbf[:, 1024:2048]),
            "dbg_sl": (nc.dram_tensor("dbg_sl", [128, 1536], F32,
                                      kind="ExternalOutput").ap(),
                       sc32[:, 0:1536]),
        }
        for name, (dd, ss) in dmp.items():
            inc("sH", 16)
            ph.op("sync", lambda s, dd=dd, ss=ss: s.dma_start(out=dd, in_=ss),
                  att=[("sH", 16)])
    ph.wt("sync", "sH", C["sH"])

    for p in phases:
        with nc.Block(p.name) as blk:
            for e in ENGS:
                fns = p.ops[e]
                if not fns:
                    continue
                def run(engine, fns=fns):
                    for f in fns:
                        f(engine)
                getattr(blk, e)(run)
    ctx.close()
    return nc


def kernel(**inputs):
    maps = host_inputs(inputs)
    nc = build_nc()
    res = run_bass_kernel_spmd(nc, maps, core_ids=list(range(8)))
    out = np.empty((B, S, E), np.float32)
    for c in range(8):
        beta, lane = c // 4, c % 4
        out[beta, lane::4, :] = np.asarray(res.results[c]["out"], np.float32).T
    return out


# version 06 (final)
# speedup vs baseline: 1.0564x; 1.0564x over previous
import os
import numpy as np
import ml_dtypes
import concourse.bass as bass
import concourse.mybir as mybir
from concourse.bass_utils import run_bass_kernel_spmd

F32 = mybir.dt.float32
BF16 = mybir.dt.bfloat16
AF = mybir.ActivationFunctionType
ALU = mybir.AluOpType
AX = mybir.AxisListType

B, S, E, H = 2, 4096, 2048, 16
D = 128
TOK = 1024
GW = 512
FC = 16
SCALE = float(1.0 / np.sqrt(128.0))
NEG = -1.0e9
MW = 897

bfd = ml_dtypes.bfloat16
N_SUB = int(os.environ.get("ATH_SUBLAYERS", "4"))
ENGS = ("tensor", "vector", "scalar", "sync", "gpsimd")


def tile_weight(w, kparts):
    K, M = w.shape
    t = w.reshape(kparts, 128, M // 128, 128).transpose(2, 1, 0, 3)
    return np.ascontiguousarray(t.astype(np.float32))


def rope_tables_for(positions):
    inv = 1.0 / (10000.0 ** (np.arange(0, D, 2, dtype=np.float64) / D))
    ang = np.concatenate([np.outer(positions, inv)] * 2, axis=1)
    cos = np.cos(ang).T
    sin = np.sin(ang).T
    sgn = np.where(np.arange(D) % 2 == 0, -1.0, 1.0)[:, None]
    return (np.ascontiguousarray(cos).astype(bfd),
            np.ascontiguousarray(sin * sgn).astype(bfd))


def masks_for(lane):
    out = np.empty((4, 128, MW), np.float32)
    kap = np.arange(128)[:, None]
    wc = np.arange(MW)[None, :]
    for r in range(4):
        dlt = 0 if r <= lane else -1
        out[r] = np.where(kap <= wc - 385 + dlt, 0.0, NEG)
    return np.ascontiguousarray(out.reshape(512, MW)).astype(bfd)


def host_inputs(inp):
    f32 = lambda a: np.asarray(a, np.float32)
    shared = {
        "wqa": tile_weight(np.ascontiguousarray(f32(inp["L0_attn_a_Wqkv"]).T), FC),
        "woa": tile_weight(np.ascontiguousarray(f32(inp["L0_attn_a_Wout"]).T), FC),
        "wqb": tile_weight(f32(inp["L0_attn_b_W_qkv"]), FC),
        "wob": tile_weight(f32(inp["L0_attn_b_W_out"]), FC),
        "wg": tile_weight(f32(inp["L0_ffn_w_gate"]), FC),
        "wu": tile_weight(f32(inp["L0_ffn_w_up"]), FC),
        "wd": tile_weight(f32(inp["L0_ffn_w_down"]), 8),
        "wsg": tile_weight(f32(inp["L0_ffn_shared_w_gate"]), FC),
        "wsu": tile_weight(f32(inp["L0_ffn_shared_w_up"]), FC),
        "wsd": tile_weight(f32(inp["L0_ffn_shared_w_down"]), FC),
        "wrt": np.ascontiguousarray(
            f32(inp["L0_ffn_w_router"]).reshape(FC, 128, 128).transpose(1, 0, 2)),
    }
    cvec = np.zeros((128, 256), np.float32)
    cvec[:, 0:128] = np.eye(128, dtype=np.float32)
    cvec[:, 128:176] = f32(inp["L0_attn_b_b_qkv"]).reshape(48, 128).T
    cvec[:, 176:192] = f32(inp["L0_attn_b_b_out"]).reshape(16, 128).T
    cvec[:, 192:193] = f32(inp["L0_ffn_expert_scale"]).reshape(128, 1)
    cvec[:, 193:201] = f32(inp["L0_ffn_amplitudes"])[..., 0]
    cvec[:, 201:209] = f32(inp["L0_ffn_amplitudes"])[..., 1]
    shared["cvec"] = cvec
    shared["identb"] = np.eye(128, dtype=np.float32).astype(bfd)

    x = f32(inp["L0_attn_a_x"])
    maps = []
    for c in range(8):
        beta, lane = c // 4, c % 4
        pos = np.arange(lane, S, 4)
        m = dict(shared)
        m["x"] = np.ascontiguousarray(x[beta, pos, :].T)
        m["cosb"], m["sinb"] = rope_tables_for(pos)
        m["masks"] = masks_for(lane)
        maps.append(m)
    return maps


def build_nc():
    import contextlib
    nc = bass.Bass("TRN2", target_bir_lowering=False,
                   detect_race_conditions=False)
    ctx = contextlib.ExitStack()

    d_x = nc.dram_tensor("x", [2048, TOK], F32, kind="ExternalInput").ap()
    d_cos = nc.dram_tensor("cosb", [128, TOK], BF16, kind="ExternalInput").ap()
    d_sin = nc.dram_tensor("sinb", [128, TOK], BF16, kind="ExternalInput").ap()
    d_msk = nc.dram_tensor("masks", [512, MW], BF16, kind="ExternalInput").ap()
    d_cvec = nc.dram_tensor("cvec", [128, 256], F32, kind="ExternalInput").ap()
    d_idb = nc.dram_tensor("identb", [128, 128], BF16, kind="ExternalInput").ap()
    d_wrt = nc.dram_tensor("wrt", [128, FC, 128], F32, kind="ExternalInput").ap()
    dw = {}
    for nm, mt, kc in (("wqa", 48, FC), ("woa", 16, FC), ("wqb", 48, FC),
                       ("wob", 16, FC), ("wg", 64, FC), ("wu", 64, FC),
                       ("wd", 16, 8), ("wsg", 16, FC), ("wsu", 16, FC),
                       ("wsd", 16, FC)):
        dw[nm] = nc.dram_tensor(nm, [mt, 128, kc, 128], F32,
                                kind="ExternalInput").ap()
    d_out = nc.dram_tensor("out", [2048, TOK], F32, kind="ExternalOutput").ap()
    cc_in = nc.dram_tensor("cc_in", [4096, TOK], BF16, kind="Internal").ap()
    cc_out = nc.dram_tensor("cc_out", [16384, TOK], BF16, kind="Internal").ap()

    sbt = lambda n, s, d: ctx.enter_context(nc.sbuf_tensor(n, s, d))

    xb = sbt("xb", [128, FC * TOK], F32)
    hb = sbt("hb", [128, FC * TOK], BF16)
    u1 = sbt("u1", [128, FC * TOK], BF16)
    u2 = sbt("u2", [128, FC * TOK], BF16)
    wbuf = sbt("wbuf", [128, 4096], BF16)
    cf = sbt("cf", [128, 256], F32)
    cbi = sbt("cbi", [128, 128], BF16)
    wrt_hl = sbt("wrt_hl", [128, 4096], BF16)
    cosb = sbt("cosb_s", [128, TOK], BF16)
    sinb = sbt("sinb_s", [128, TOK], BF16)
    sc32 = sbt("sc32", [128, 3712], F32)
    scbf = sbt("scbf", [128, 2048], BF16)
    rows = sbt("rows", [128, 1024], F32)
    mixb = sbt("mixb", [128, 8], F32)
    onesr = sbt("onesr", [1, 128], F32)
    ones32 = sbt("ones32", [128, 1], F32)
    onesb16 = sbt("onesb16", [128, 1], BF16)
    epsb = sbt("epsb", [128, 1], F32)

    U1_Q = 0
    U1_ACC = 0
    U1_CK = 8192
    U2_K = 0
    U2_EXP = 2048
    U2_V = 4096
    U2_MASK = 6144
    U2_VS = 9984
    U2_VT0 = 11008
    U2_VT1 = 12032
    U2_KV = 13056
    U2_RT = 14080
    SC_XSQ, SC_T, SC_H32 = 0, 512, 1024
    SC_PROB = 1536
    SC_WFT = 2560
    R_MEAN, R_T1, R_VAR = (0, 0), (32, 0), (64, 0)
    R_SD, R_RSTD = (32, 512), (0, 512)

    def row(slot, w=512):
        p, c0 = slot
        return rows[p:p + 1, c0:c0 + w]

    psA = ctx.enter_context(nc.psum_tensor("psA", [128, 1024], F32))
    psB = ctx.enter_context(nc.psum_tensor("psB", [128, 1024], F32))
    psAV1 = ctx.enter_context(nc.psum_tensor("psAV1", [128, 512], F32))
    psAV2 = ctx.enter_context(nc.psum_tensor("psAV2", [128, 512], F32))
    psX = ctx.enter_context(nc.psum_tensor("psX", [128, 1024], F32))
    psS1 = psX[0:1, 0:512]
    psS2 = psX[0:1, 512:1024]
    psXb = psX[:].bitcast(BF16)
    psT = [psXb[:, 0:128], psXb[:, 1024:1152]]
    GEN = [psA[:, 0:512], psA[:, 512:1024], psB[:, 0:512], psB[:, 512:1024]]

    sems = {}
    C = {}
    for s in ("sW", "sWf", "sH", "sP1", "sP2", "sV", "sS", "sES", "sC", "sC2"):
        sems[s] = ctx.enter_context(nc.semaphore(s))
        C[s] = 0

    def inc(name, n=1):
        C[name] += n
        return C[name]

    phases = []

    class Ph:
        def __init__(self, name):
            self.name = name
            self.ops = {e: [] for e in ENGS}
            phases.append(self)

        def add(self, eng, fn):
            self.ops[eng].append(fn)

        def wt(self, eng, sname, val):
            if val and val > 0:
                self.ops[eng].append(
                    lambda e, s=sems[sname], v=val: e.wait_ge(s, v))

        def op(self, eng, build, att=(), selfwait=None):
            rs = [(sems[n], a) for (n, a) in att]
            sw = (sems[selfwait[0]], selfwait[1]) if selfwait else None
            def run(e, b=build, rs=rs, sw=sw):
                ins = b(e)
                if rs:
                    ins.then_inc(*rs[0])
                    if len(rs) > 1 and sw is not None:
                        e.wait_ge(*sw)
                    for (h, a) in rs[1:]:
                        e.nop().then_inc(h, a)
            self.add(eng, run)

    wf_n = [0]

    def sched_strips(ph, striplist):
        recs = []
        for (nm, mi, kc) in striplist:
            wf_n[0] += 1
            recs.append(dict(nm=nm, mi=mi, kc=kc, slot=(wf_n[0] - 1) % 2,
                             sWf_need=wf_n[0] - 2, sW_after=inc("sW", 16)))
        for r in recs:
            ph.wt("gpsimd", "sWf", r["sWf_need"])
            dst = wbuf[:, r["slot"] * 2048: r["slot"] * 2048 + r["kc"] * 128]
            ph.op("gpsimd",
                  lambda g, d=dst.rearrange("p (k c) -> p k c", c=128),
                  s=dw[r["nm"]][r["mi"]]:
                  g.dma_start(out=d, in_=s),
                  att=[("sW", 16)])
        return recs

    def wslice(rec, kch):
        return wbuf[:, rec["slot"] * 2048 + kch * 128:
                    rec["slot"] * 2048 + kch * 128 + 128]

    def hslc(kc, g, w=512, off=0):
        return hb[:, kc * TOK + g * GW + off: kc * TOK + g * GW + off + w]

    def xslc(kc, g):
        return xb[:, kc * TOK + g * GW: kc * TOK + g * GW + 512]

    ph = Ph("init")
    loads = [
        (xb[:].rearrange("p (k t) -> p k t", t=TOK),
         d_x.rearrange("(k p) t -> p k t", p=128)),
        (cf[:], d_cvec), (cbi[:], d_idb), (cosb[:], d_cos), (sinb[:], d_sin),
        (u2[:, U2_MASK:U2_MASK + 4 * MW].rearrange("p (r w) -> p r w", w=MW),
         d_msk.rearrange("(r p) w -> p r w", p=128)),
        (sc32[:, 0:2048].rearrange("p (k c) -> p k c", c=128), d_wrt),
    ]
    for dst, src in loads:
        inc("sH", 16)
        ph.op("sync", lambda s, d=dst, i=src: s.dma_start(out=d, in_=i),
              att=[("sH", 16)])
    ph.wt("sync", "sH", C["sH"])
    ph.wt("scalar", "sH", C["sH"])
    ph.wt("tensor", "sH", C["sH"])
    ph.wt("vector", "sH", C["sH"])
    ph.add("vector", lambda v: v.memset(onesr[:], 1.0))
    ph.add("vector", lambda v: v.memset(ones32[:], 1.0))
    ph.add("vector", lambda v: v.memset(onesb16[:], 1.0))
    ph.add("vector", lambda v: v.memset(epsb[:], 1e-5))

    ph = Ph("init2")
    amp0 = cf[:, 193:201]
    amp1 = cf[:, 201:209]
    sqa = sc32[:, 3616:3624]
    sqb = sc32[:, 3624:3632]
    p2 = sc32[:, 3632:3640]
    rs1 = sc32[:, 3640:3641]
    rcp1 = sc32[:, 3641:3642]
    s1 = inc("sS")
    s2 = inc("sS")
    ph.op("scalar", lambda sc: sc.activation(sqa, amp0, AF.Square),
          att=[("sS", 1)])
    ph.op("scalar", lambda sc: sc.activation(sqb, amp1, AF.Square),
          att=[("sS", 1)])
    ph.add("vector", lambda v: v.tensor_copy(wrt_hl[:, 0:2048], sc32[:, 0:2048]))
    ph.add("vector", lambda v: v.tensor_tensor(
        wrt_hl[:, 2048:4096], sc32[:, 0:2048], wrt_hl[:, 0:2048], ALU.subtract))
    ph.wt("vector", "sS", s2)
    v0 = inc("sV")
    ph.op("vector", lambda v: v.tensor_add(p2, sqa, sqb), att=[("sV", 1)])
    ph.wt("vector", "sV", v0)
    v1 = inc("sV")
    ph.op("vector", lambda v: v.reduce_sum(rs1, p2, axis=AX.X), att=[("sV", 1)])
    ph.wt("vector", "sV", v1)
    v1b = inc("sV")
    ph.op("vector", lambda v: v.tensor_scalar(rs1, rs1, 1e-8, None, ALU.add),
          att=[("sV", 1)])
    ph.wt("vector", "sV", v1b)
    v2 = inc("sV")
    ph.op("vector", lambda v: v.reciprocal(rcp1, rs1), att=[("sV", 1)])
    ph.wt("vector", "sV", v2)
    ph.add("vector", lambda v: v.tensor_scalar(mixb[:], p2, rcp1, None, ALU.mult))

    def emit_ln(tag, with_router):
        ph = Ph("ln_" + tag)
        vbar = C["sV"]
        ph.wt("scalar", "sV", vbar)
        ph.wt("tensor", "sV", vbar)
        pl = {}
        for g in range(2):
            pl[g] = dict(sq=[inc("sS") for _ in range(FC)], sqrt=inc("sS"))
        for g in range(2):
            pl[g]["mm"] = [inc("sP1") for _ in range(FC)]
            pl[g]["bcast"] = inc("sP1")
            if with_router:
                pl[g]["rmm"] = [inc("sP1") for _ in range(FC)]
        if with_router:
            pl["router_done"] = inc("sP1")
        for g in range(2):
            pl[g]["rows_a"] = inc("sV")
            pl[g]["rows_c"] = inc("sV")
            pl[g]["hn"] = [inc("sV") for _ in range(FC)]

        for g in range(2):
            p = pl[g]
            for kc in range(FC):
                if kc > 0:
                    ph.wt("scalar", "sP1", p["mm"][kc - 1])
                elif g == 1:
                    ph.wt("scalar", "sP1", pl[0]["mm"][FC - 1])
                ph.op("scalar", lambda sc, x=xslc(kc, g): sc.activation(
                    sc32[:, SC_XSQ:SC_XSQ + 512], x, AF.Square), att=[("sS", 1)])
            ph.wt("scalar", "sV", p["rows_a"])
            ph.op("scalar", lambda sc: sc.activation(
                row(R_SD), row(R_VAR), AF.Sqrt, bias=epsb[0:1, 0:1]), att=[("sS", 1)])

        for g in range(2):
            p = pl[g]
            if g == 1:
                ph.wt("tensor", "sV", pl[0]["rows_a"])
            for kc in range(FC):
                ph.add("tensor", lambda t, x=xslc(kc, g), k=kc: t.matmul(
                    psS1, ones32[:], x, start=(k == 0), stop=(k == FC - 1),
                    skip_group_check=True))
                ph.wt("tensor", "sS", p["sq"][kc])
                ph.op("tensor", lambda t, k=kc: t.matmul(
                    psS2, ones32[:], sc32[:, SC_XSQ:SC_XSQ + 512],
                    start=(k == 0), stop=(k == FC - 1), skip_group_check=True),
                    att=[("sP1", 1)])
            ph.wt("tensor", "sV", p["rows_c"])
            ph.add("tensor", lambda t: t.matmul(
                psA[:, 0:512], onesr[:], row(R_MEAN), start=True, stop=True,
                skip_group_check=True))
            ph.op("tensor", lambda t: t.matmul(
                psA[:, 512:1024], onesr[:], row(R_RSTD), start=True, stop=True,
                skip_group_check=True), att=[("sP1", 1)])
            if with_router:
                for kc in range(FC):
                    ph.wt("tensor", "sV", p["hn"][kc])
                    for ts_ in range(4):
                        hh = hslc(kc, g, 128, ts_ * 128)
                        hl = scbf[:, (kc % 2) * 512 + ts_ * 128:
                                  (kc % 2) * 512 + ts_ * 128 + 128]
                        pq = (psAV1 if g == 0 else psAV2)[:, ts_ * 128:
                                                          ts_ * 128 + 128]
                        ph.add("tensor", lambda t, pq=pq, hh=hh, k=kc,
                               s0=(kc == 0 and ts_ == 0): t.matmul(
                            pq, hh, wrt_hl[:, k * 128: k * 128 + 128],
                            start=s0, stop=False, skip_group_check=True))
                        ph.add("tensor", lambda t, pq=pq, hh=hh, k=kc: t.matmul(
                            pq, hh, wrt_hl[:, 2048 + k * 128: 2048 + k * 128 + 128],
                            start=False, stop=False, skip_group_check=True))
                        att = [("sP1", 1)] if ts_ == 3 else ()
                        ph.op("tensor", lambda t, pq=pq, hl=hl, k=kc: t.matmul(
                            pq, hl, wrt_hl[:, k * 128: k * 128 + 128],
                            start=False, stop=(k == FC - 1),
                            skip_group_check=True), att=att)
        if with_router:
            ph.op("tensor", lambda t: t.nop(), att=[("sP1", 1)])

        for g in range(2):
            p = pl[g]
            ph.wt("vector", "sP1", p["mm"][FC - 1])
            ph.add("vector", lambda v: v.tensor_scalar(
                row(R_MEAN), psS1, 1.0 / 2048.0, None, ALU.mult))
            ph.add("vector", lambda v: v.tensor_mul(
                row(R_T1), row(R_MEAN), row(R_MEAN)))
            ph.op("vector", lambda v: v.scalar_tensor_tensor(
                row(R_VAR), psS2, 1.0 / 2048.0, row(R_T1), ALU.mult,
                ALU.subtract), att=[("sV", 1)])
            ph.wt("vector", "sS", p["sqrt"])
            ph.op("vector", lambda v: v.reciprocal(row(R_RSTD), row(R_SD)),
                  att=[("sV", 1)])
            ph.wt("vector", "sP1", p["bcast"])
            for kc in range(FC):
                tt = sc32[:, SC_T:SC_T + 512]
                if with_router:
                    h32 = sc32[:, SC_H32:SC_H32 + 512]
                    if kc >= 2:
                        ph.wt("vector", "sP1", p["rmm"][kc - 2])
                    ph.add("vector", lambda v, x=xslc(kc, g): v.tensor_tensor(
                        tt, x, psA[:, 0:512], ALU.subtract))
                    ph.add("vector", lambda v: v.tensor_tensor(
                        h32, tt, psA[:, 512:1024], ALU.mult))
                    ph.add("vector", lambda v, h=hslc(kc, g):
                           v.tensor_copy(h, h32))
                    ph.op("vector", lambda v, h=hslc(kc, g), k=kc:
                          v.tensor_tensor(
                              scbf[:, (k % 2) * 512:(k % 2) * 512 + 512],
                              h32, h, ALU.subtract), att=[("sV", 1)])
                else:
                    ph.add("vector", lambda v, x=xslc(kc, g): v.tensor_tensor(
                        tt, x, psA[:, 0:512], ALU.subtract))
                    ph.op("vector", lambda v, h=hslc(kc, g): v.tensor_tensor(
                        h, tt, psA[:, 512:1024], ALU.mult), att=[("sV", 1)])
        return pl

    def emit_qkv(tag, wname, rope, bias, pl, ccsem):
        ph = Ph("qkv_" + tag)
        m_order = list(range(16, 32)) + list(range(32, 48)) + list(range(0, 16))
        recs = sched_strips(ph, [(wname, mi, FC) for mi in m_order])

        steps = []
        for j, mi in enumerate(m_order):
            for g in range(2):
                kind = "k" if 16 <= mi < 32 else ("v" if mi >= 32 else "q")
                steps.append(dict(mi=mi, g=g, j=j, rec=recs[j], kind=kind,
                                  gen=(2 * j + g) % 4, t=len(steps)))
        all_tp_vals = []
        for st in steps:
            st["mm_done"] = inc("sP1")
            if st["g"] == 1:
                st["wf"] = inc("sWf")
            if st["kind"] == "v" and st["g"] == 1:
                st["tp"] = [inc("sP1") for _ in range(8)]
        for st in steps:
            if st["kind"] == "v":
                st["ev"] = ("sES", inc("sES"))
            elif rope:
                st["plain"] = inc("sS")
        for st in steps:
            if st["kind"] == "v":
                if st["g"] == 1:
                    st["tpe"] = [inc("sV") for _ in range(8)]
            else:
                if rope:
                    st["v3"] = [inc("sV") for _ in range(3)]
                st["ev"] = ("sV", inc("sV"))
        for st in steps:
            if st["kind"] == "k":
                st["kv_sH"] = inc("sH", 16)
            elif st["kind"] == "v" and st["g"] == 1:
                st["vt_sH"] = inc("sH", 16)
        kv_last = C["sH"]
        cc_done = inc(ccsem)

        tp_seen = []
        ph.wt("tensor", "sV", pl[1]["hn"][FC - 1])
        for st in steps:
            if st["g"] == 0:
                ph.wt("tensor", "sW", st["rec"]["sW_after"])
            if st["t"] >= 4:
                pv = steps[st["t"] - 4]
                ph.wt("tensor", pv["ev"][0], pv["ev"][1])
            gen = GEN[st["gen"]]
            for kc in range(FC):
                last = (kc == FC - 1)
                att = ([("sP1", 1)] + ([("sWf", 1)] if st["g"] == 1 else [])) \
                    if last else ()
                ph.op("tensor", lambda t, gen=gen, w=wslice(st["rec"], kc),
                      h=hslc(kc, st["g"]), k=kc: t.matmul(
                          gen, w, h, start=(k == 0), stop=(k == FC - 1),
                          skip_group_check=True), att=att,
                      selfwait=("sP1", st["mm_done"]) if last else None)
            if st["kind"] == "v" and st["g"] == 1:
                ph.wt("tensor", "sES", st["ev"][1])
                for c in range(8):
                    gidx = len(tp_seen)
                    if gidx >= 2:
                        ph.wt("tensor", "sV", tp_seen[gidx - 2])
                    ph.op("tensor", lambda t, c=c: t.transpose(
                        psT[c % 2],
                        u2[:, U2_VS + c * 128: U2_VS + c * 128 + 128],
                        cbi[:]), att=[("sP1", 1)])
                    tp_seen.append(st["tpe"][c])

        prev_v = [None]
        last_rt = {0: None, 1: None}
        for st in steps:
            if st["kind"] == "v":
                ph.wt("scalar", "sP1", st["mm_done"])
                if prev_v[0] is not None and st["g"] == 0:
                    ph.wt("scalar", "sP1", prev_v[0]["tp"][7])
                dst = u2[:, U2_VS + st["g"] * 512: U2_VS + st["g"] * 512 + 512]
                if bias:
                    ph.op("scalar", lambda sc, d=dst, gen=GEN[st["gen"]],
                          b=cf[:, 128 + st["mi"]:129 + st["mi"]]:
                          sc.activation(d, gen, AF.Identity, bias=b),
                          att=[("sES", 1)])
                else:
                    ph.op("scalar", lambda sc, d=dst, gen=GEN[st["gen"]]:
                          sc.activation(d, gen, AF.Identity), att=[("sES", 1)])
                if st["g"] == 1:
                    prev_v[0] = st
            elif rope:
                ph.wt("scalar", "sP1", st["mm_done"])
                sl = st["t"] % 2
                if last_rt[sl] is not None:
                    ph.wt("scalar", last_rt[sl]["ev"][0], last_rt[sl]["ev"][1])
                dst = u2[:, U2_RT + sl * 512: U2_RT + sl * 512 + 512]
                ph.op("scalar", lambda sc, d=dst, gen=GEN[st["gen"]]:
                      sc.activation(d, gen, AF.Identity), att=[("sS", 1)])
                last_rt[sl] = st

        tb = scbf[:, 0:512]
        for st in steps:
            t_i = st["t"]
            if st["kind"] == "v":
                if st["g"] == 1:
                    vt = U2_VT0 if (st["j"] % 2 == 0) else U2_VT1
                    pv2 = [s2 for s2 in steps
                           if s2["kind"] == "v" and s2["g"] == 1
                           and s2["j"] == st["j"] - 2]
                    for c in range(8):
                        ph.wt("vector", "sP1", st["tp"][c])
                        if c == 0 and pv2:
                            ph.wt("vector", "sH", pv2[0]["vt_sH"])
                        ph.op("vector", lambda v, c=c, vt=vt: v.tensor_copy(
                            u2[:, vt + c * 128: vt + c * 128 + 128], psT[c % 2]),
                            att=[("sV", 1)])
                continue
            if rope:
                ph.wt("vector", "sS", st["plain"])
                shuf = u2[:, U2_RT + 1024 + (t_i % 2) * 512:
                          U2_RT + 1024 + (t_i % 2) * 512 + 512]
                plain = u2[:, U2_RT + (t_i % 2) * 512:
                           U2_RT + (t_i % 2) * 512 + 512]
                ph.op("vector", lambda v, s2=shuf, p_=plain:
                      v.stream_shuffle(s2, p_, [p ^ 1 for p in range(32)]),
                      att=[("sV", 1)])
                cs = cosb[:, st["g"] * GW: st["g"] * GW + 512]
                sn = sinb[:, st["g"] * GW: st["g"] * GW + 512]
                ph.op("vector", lambda v, p_=plain, cs=cs: v.tensor_tensor(
                    tb, p_, cs, ALU.mult), att=[("sV", 1)])
                ph.wt("vector", "sV", st["v3"][0])
                ph.op("vector", lambda v, s2=shuf, sn=sn: v.tensor_tensor(
                    s2, s2, sn, ALU.mult), att=[("sV", 1)])
                ph.wt("vector", "sV", st["v3"][2])
            if st["kind"] == "k":
                dst = u2[:, U2_KV + (t_i % 2) * 512: U2_KV + (t_i % 2) * 512 + 512]
                if t_i >= 2 and steps[t_i - 2]["kind"] == "k":
                    ph.wt("vector", "sH", steps[t_i - 2]["kv_sH"])
            else:
                dst = u1[:, U1_Q + st["mi"] * TOK + st["g"] * GW:
                         U1_Q + st["mi"] * TOK + st["g"] * GW + 512]
            if rope:
                ph.op("vector", lambda v, d=dst,
                      s2=u2[:, U2_RT + 1024 + (t_i % 2) * 512:
                            U2_RT + 1024 + (t_i % 2) * 512 + 512]:
                      v.tensor_add(d, tb, s2), att=[("sV", 1)])
            else:
                ph.wt("vector", "sP1", st["mm_done"])
                if bias:
                    ph.op("vector", lambda v, d=dst, gen=GEN[st["gen"]],
                          b=cf[:, 128 + st["mi"]:129 + st["mi"]]:
                          v.tensor_scalar(d, gen, b, None, ALU.add),
                          att=[("sV", 1)])
                else:
                    ph.op("vector", lambda v, d=dst, gen=GEN[st["gen"]]:
                          v.tensor_copy(d, gen), att=[("sV", 1)])

        for st in steps:
            if st["kind"] == "k":
                ph.wt("sync", st["ev"][0], st["ev"][1])
                ph.op("sync", lambda s, st=st: s.dma_start(
                    out=cc_in[(st["mi"] - 16) * 128:(st["mi"] - 16) * 128 + 128,
                              st["g"] * GW: st["g"] * GW + 512],
                    in_=u2[:, U2_KV + (st["t"] % 2) * 512:
                           U2_KV + (st["t"] % 2) * 512 + 512]),
                    att=[("sH", 16)])
            elif st["kind"] == "v" and st["g"] == 1:
                ph.wt("sync", "sV", st["tpe"][7])
                h = st["mi"] - 32
                hf = h // 8
                colv = (h * 128) % 1024
                base = (2048 + hf) * TOK + colv
                vt = U2_VT0 if (st["j"] % 2 == 0) else U2_VT1
                ph.op("sync", lambda s, base=base, vt=vt: s.dma_start(
                    out=bass.AP(cc_in.tensor, base,
                                [[2 * TOK, 128], [256 * TOK, 8], [1, 128]]),
                    in_=u2[:, vt:vt + 1024].rearrange("p (c f) -> p c f", f=128)),
                    att=[("sH", 16)])

        ph.wt("gpsimd", "sH", kv_last)
        ph.op("gpsimd", lambda g: g.collective_compute(
            "AllGather", ALU.bypass,
            replica_groups=[[0, 1, 2, 3], [4, 5, 6, 7]],
            ins=[cc_in.opt()], outs=[cc_out.opt()]), att=[(ccsem, 1)])

        qev = {}
        for st in steps:
            if st["kind"] == "q":
                qev[(st["g"], st["mi"])] = st["ev"][1]
        return dict(cc_done=cc_done, steps=steps, qev=qev, ccsem=ccsem)

    def emit_attention(tag, qkv_info):
        ph = Ph("att_" + tag)
        cc_done = qkv_info["cc_done"]

        segs = []
        steps = []
        for g in range(2):
            for hp in range(8):
                for r in range(4):
                    for sh in range(g + 1):
                        seg = dict(g=g, hp=hp, r=r, sh=sh, gseg=len(segs))
                        segs.append(seg)
                        for m in range(4):
                            c = sh * 4 + m
                            steps.append(dict(
                                g=g, hp=hp, r=r, c=c, seg=seg, i=len(steps),
                                partial=(c >= 4 * g), mloc=c - 4 * g))
        byp = {}
        for i, st in enumerate(steps):
            byp.setdefault((st["g"], st["hp"]), []).append(i)
        for kk, idxs in byp.items():
            steps[idxs[0]]["pstart"] = True
            steps[idxs[-1]]["pstop"] = True
        for st in steps:
            st.setdefault("pstart", False)
            st.setdefault("pstop", False)
        for sg in segs:
            sg["last_step"] = max(st["i"] for st in steps if st["seg"] is sg)

        n = len(steps)
        for sg in segs:
            sg["ld"] = [inc("sH", 16) for _ in range(3)]
        pe_order = []
        for i in range(n):
            pe_order.append(("sc", i))
            if i >= 2:
                pe_order.append(("av", i - 2))
                if steps[i - 2]["pstop"]:
                    pe_order.append(("ep", (steps[i - 2]["g"], steps[i - 2]["hp"])))
        for i in (n - 2, n - 1):
            pe_order.append(("av", i))
            if steps[i]["pstop"]:
                pe_order.append(("ep", (steps[i]["g"], steps[i]["hp"])))
        epil = {}
        for kind, a in pe_order:
            if kind == "sc":
                steps[a]["sc_done"] = inc("sP1")
            elif kind == "av":
                steps[a]["av_done"] = inc("sP2")
            else:
                epil.setdefault(a, {})["bcast"] = inc("sP1")
        for st in steps:
            st["exp_done"] = inc("sS")
        for st in steps:
            if st["partial"]:
                st["mask_done"] = inc("sV")
            if st["pstop"]:
                ep = epil[(st["g"], st["hp"])]
                ep["recip"] = inc("sV")
                ep["fin"] = inc("sV")

        def prev_pair(pk):
            g0, hp0 = pk
            if hp0 > 0:
                return (g0, hp0 - 1)
            return (g0 - 1, 7) if g0 > 0 else None

        ph.wt("sync", qkv_info["ccsem"], cc_done)
        for sg in segs:
            gseg = sg["gseg"]
            if gseg >= 2:
                ph.wt("sync", "sP2",
                      steps[segs[gseg - 2]["last_step"]]["av_done"])
            par = gseg % 2
            h1 = sg["hp"] * 2
            r = sg["r"]
            colr = sg["sh"] * 512
            for hh, koff in ((h1, 0), (h1 + 1, 512)):
                ph.op("sync", lambda s, hh=hh, koff=koff, r=r, colr=colr,
                      par=par: s.dma_start(
                          out=u2[:, U2_K + par * 1024 + koff:
                                 U2_K + par * 1024 + koff + 512],
                          in_=cc_out[r * 4096 + hh * 128:
                                     r * 4096 + hh * 128 + 128,
                                     colr: colr + 512]), att=[("sH", 16)])
            hp = sg["hp"]
            hf = (hp * 256) // 1024
            colv = (hp * 256) % 1024
            base = (r * 4096 + 2048 + hf) * TOK + colv + 2 * colr * TOK
            ph.op("sync", lambda s, base=base, par=par: s.dma_start(
                out=u2[:, U2_V + par * 1024: U2_V + par * 1024 + 1024]
                .rearrange("p (c f) -> p c f", f=256),
                in_=bass.AP(cc_out.tensor, base,
                            [[2 * TOK, 128], [256 * TOK, 4], [1, 256]])),
                att=[("sH", 16)])

        def emit_sc(i):
            st = steps[i]
            sg = st["seg"]
            par = i % 2
            pss = psA if par == 0 else psB
            if st["pstart"]:
                ph.wt("tensor", "sV",
                      qkv_info["qev"][(st["g"], st["hp"] * 2 + 1)])
            ph.wt("tensor", "sH", sg["ld"][1])
            if i >= 2:
                ph.wt("tensor", "sS", steps[i - 2]["exp_done"])
            kt = st["c"] % 4
            kpar = sg["gseg"] % 2
            h1 = st["hp"] * 2
            q1 = u1[:, h1 * TOK + st["g"] * GW: h1 * TOK + st["g"] * GW + 512]
            q2 = u1[:, (h1 + 1) * TOK + st["g"] * GW:
                    (h1 + 1) * TOK + st["g"] * GW + 512]
            k1 = u2[:, U2_K + kpar * 1024 + kt * 128:
                    U2_K + kpar * 1024 + kt * 128 + 128]
            k2 = u2[:, U2_K + kpar * 1024 + 512 + kt * 128:
                    U2_K + kpar * 1024 + 512 + kt * 128 + 128]
            ph.add("tensor", lambda t, pss=pss, k1=k1, q1=q1: t.matmul(
                pss[:, 0:512], k1, q1, start=True, stop=True,
                skip_group_check=True))
            ph.op("tensor", lambda t, pss=pss, k2=k2, q2=q2: t.matmul(
                pss[:, 512:1024], k2, q2, start=True, stop=True,
                skip_group_check=True), att=[("sP1", 1)])

        def emit_av(i):
            st = steps[i]
            sg = st["seg"]
            par = i % 2
            ph.wt("tensor", "sS", st["exp_done"])
            ph.wt("tensor", "sH", sg["ld"][2])
            if st["pstart"]:
                pk2 = prev_pair((st["g"], st["hp"]))
                if pk2 is not None and pk2 in epil:
                    ph.wt("tensor", "sV", epil[pk2]["fin"])
            kt = st["c"] % 4
            kpar = sg["gseg"] % 2
            ex = u2[:, U2_EXP + par * 1024: U2_EXP + par * 1024 + 1024]
            vb = u2[:, U2_V + kpar * 1024: U2_V + kpar * 1024 + 1024]
            ph.add("tensor", lambda t, v1=vb[:, kt * 256: kt * 256 + 128],
                   e1=ex[:, 0:512], st=st: t.matmul(
                       psAV1[:], v1, e1, start=st["pstart"], stop=st["pstop"],
                       skip_group_check=True))
            ph.add("tensor", lambda t, v2=vb[:, kt * 256 + 128: kt * 256 + 256],
                   e2=ex[:, 512:1024], st=st: t.matmul(
                       psAV2[:], v2, e2, start=st["pstart"], stop=st["pstop"],
                       skip_group_check=True))
            ph.add("tensor", lambda t, e1=ex[:, 0:512], st=st: t.matmul(
                psS1, onesb16[:], e1, start=st["pstart"], stop=st["pstop"],
                skip_group_check=True))
            ph.op("tensor", lambda t, e2=ex[:, 512:1024], st=st: t.matmul(
                psS2, onesb16[:], e2, start=st["pstart"], stop=st["pstop"],
                skip_group_check=True), att=[("sP2", 1)])

        def emit_ep(pk):
            ep = epil[pk]
            ph.wt("tensor", "sV", ep["recip"])
            ph.add("tensor", lambda t: t.matmul(
                psX[:, 0:512], onesr[:], rows[0:1, 0:512], start=True,
                stop=True, skip_group_check=True))
            ph.op("tensor", lambda t: t.matmul(
                psX[:, 512:1024], onesr[:], rows[0:1, 512:1024], start=True,
                stop=True, skip_group_check=True), att=[("sP1", 1)])

        for kind, a in pe_order:
            (emit_sc if kind == "sc" else emit_av if kind == "av"
             else emit_ep)(a)

        for i, st in enumerate(steps):
            par = i % 2
            pss = psA if par == 0 else psB
            if st["partial"]:
                ph.wt("scalar", "sV", st["mask_done"])
            else:
                ph.wt("scalar", "sP1", st["sc_done"])
            if i >= 2:
                ph.wt("scalar", "sP2", steps[i - 2]["av_done"])
            ph.op("scalar", lambda sc, pss=pss, par=par: sc.activation(
                u2[:, U2_EXP + par * 1024: U2_EXP + par * 1024 + 1024],
                pss[:, 0:1024], AF.Exp, scale=SCALE), att=[("sS", 1)])

        rc1 = sc32[:, SC_PROB:SC_PROB + 512]
        rc2 = sc32[:, SC_PROB + 512:SC_PROB + 1024]
        for i, st in enumerate(steps):
            par = i % 2
            pss = psA if par == 0 else psB
            if st["partial"]:
                ph.wt("vector", "sP1", st["sc_done"])
                m = st["mloc"]
                wm = 128 * (m + 1)
                r = st["seg"]["r"]
                mask = u2[:, U2_MASK + r * MW + 385 - 128 * m:
                          U2_MASK + r * MW + 385 - 128 * m + wm]
                pv = pss[:, 0:1024].rearrange("p (h w) -> p h w", w=512)
                ph.op("vector", lambda v, pv=pv, mask=mask, wm=wm:
                      v.tensor_tensor(
                          pv[:, :, 0:wm], pv[:, :, 0:wm],
                          mask.unsqueeze(1).broadcast_to([128, 2, wm]),
                          ALU.add), att=[("sV", 1)])
            if st["pstop"]:
                ep = epil[(st["g"], st["hp"])]
                ph.wt("vector", "sP2", st["av_done"])
                ph.add("vector", lambda v: v.reciprocal(rows[0:1, 0:512], psS1))
                ph.op("vector", lambda v: v.reciprocal(rows[0:1, 512:1024], psS2),
                      att=[("sV", 1)])
                ph.wt("vector", "sP1", ep["bcast"])
                ph.add("vector", lambda v: v.tensor_copy(rc1, psX[:, 0:512]))
                ph.add("vector", lambda v: v.tensor_copy(rc2, psX[:, 512:1024]))
                h1 = st["hp"] * 2
                d1 = hb[:, h1 * TOK + st["g"] * GW: h1 * TOK + st["g"] * GW + 512]
                d2 = hb[:, (h1 + 1) * TOK + st["g"] * GW:
                        (h1 + 1) * TOK + st["g"] * GW + 512]
                ph.add("vector", lambda v, d1=d1: v.tensor_tensor(
                    d1, psAV1[:], rc1, ALU.mult))
                ph.op("vector", lambda v, d2=d2: v.tensor_tensor(
                    d2, psAV2[:], rc2, ALU.mult), att=[("sV", 1)])
        return {0: epil[(0, 7)]["fin"], 1: epil[(1, 7)]["fin"]}

    def emit_outproj(tag, wname, bias, fins):
        ph = Ph("out_" + tag)
        recs = sched_strips(ph, [(wname, mi, FC) for mi in range(16)])
        steps = []
        for j in range(16):
            for g in range(2):
                steps.append(dict(mi=j, g=g, rec=recs[j], gen=len(steps) % 4))
        for st in steps:
            st["mm"] = inc("sP1")
            if st["g"] == 1:
                st["wf"] = inc("sWf")
        for st in steps:
            st["ev"] = inc("sV")
        fin_waited = [False, False]
        for i, st in enumerate(steps):
            if not fin_waited[st["g"]]:
                ph.wt("tensor", "sV", fins[st["g"]])
                fin_waited[st["g"]] = True
            if st["g"] == 0:
                ph.wt("tensor", "sW", st["rec"]["sW_after"])
            if i >= 4:
                ph.wt("tensor", "sV", steps[i - 4]["ev"])
            gen = GEN[st["gen"]]
            for kc in range(FC):
                last = (kc == FC - 1)
                att = ([("sP1", 1)] + ([("sWf", 1)] if st["g"] == 1 else [])) \
                    if last else ()
                ph.op("tensor", lambda t, gen=gen, w=wslice(st["rec"], kc),
                      h=hslc(kc, st["g"]), k=kc: t.matmul(
                          gen, w, h, start=(k == 0), stop=(k == FC - 1),
                          skip_group_check=True), att=att,
                      selfwait=("sP1", st["mm"]) if last else None)
        for st in steps:
            ph.wt("vector", "sP1", st["mm"])
            xs = xb[:, st["mi"] * TOK + st["g"] * GW:
                    st["mi"] * TOK + st["g"] * GW + 512]
            if bias:
                ph.op("vector", lambda v, xs=xs, gen=GEN[st["gen"]],
                      b=cf[:, 176 + st["mi"]:177 + st["mi"]]:
                      v.scalar_tensor_tensor(xs, gen, b, xs, ALU.add, ALU.add),
                      att=[("sV", 1)])
            else:
                ph.op("vector", lambda v, xs=xs, gen=GEN[st["gen"]]:
                      v.tensor_add(xs, xs, gen), att=[("sV", 1)])

    def emit_router_ep(tag, router_done):
        ph = Ph("rt_" + tag)
        prob = sc32[:, SC_PROB:SC_PROB + 1024]
        wcur = sc32[:, SC_WFT:SC_WFT + 1024]
        tmp = sc32[:, 0:1024]
        wfts = scbf[:, 0:1024]
        mxb = scbf[:, 1024:2048]
        negm = sc32[:, 3584:3592]
        rsum = sc32[:, 3592:3600]
        rcp = sc32[:, 3600:3608]
        thr = sc32[:, 3608:3616]
        P3 = lambda a_: a_.rearrange("p (a e) -> p a e", e=128)
        bc8 = lambda a_: a_.unsqueeze(2).broadcast_to([128, 8, 128])

        v_ms = inc("sV")
        v_mx = [inc("sV") for _ in range(8)]
        v_evac = inc("sV")
        v_negm = inc("sV")
        v_sub = inc("sV")
        v_sum = inc("sV")
        v_rcp = inc("sV")
        v_thr = [inc("sV") for _ in range(4)]
        v_ssum = inc("sV")
        v_rcp2 = inc("sV")
        v_wf = inc("sV")
        v_wft = [inc("sV") for _ in range(8)]
        v_ck = [inc("sV") for _ in range(16)]
        s_exp = inc("sS")
        t_tp = [inc("sP1") for _ in range(8)]
        t_ck = [inc("sP1") for _ in range(16)]

        ph.op("vector", lambda v: v.memset(mxb, 1.0), att=[("sV", 1)])
        ph.wt("vector", "sV", v_ms)
        for k in range(8):
            ph.op("vector", lambda v, k=k: v.tensor_scalar(
                mxb[:, k * 128: k * 128 + 128],
                mxb[:, k * 128: k * 128 + 128],
                mixb[:, k:k + 1], None, ALU.mult), att=[("sV", 1)])
        ph.wt("vector", "sV", v_mx[-1])

        ph.wt("vector", "sP1", router_done)
        ph.add("vector", lambda v: v.tensor_copy(prob[:, 0:512], psAV1[:]))
        ph.op("vector", lambda v: v.tensor_copy(prob[:, 512:1024], psAV2[:]),
              att=[("sV", 1)])
        ph.op("vector", lambda v: v.reduce_max(
            negm.unsqueeze(2), P3(prob), axis=AX.X, negate=True),
            att=[("sV", 1)])
        ph.wt("vector", "sV", v_negm)
        ph.op("vector", lambda v: v.tensor_tensor(
            P3(prob), P3(prob), bc8(negm), ALU.add), att=[("sV", 1)])
        ph.wt("vector", "sS", s_exp)
        ph.op("vector", lambda v: v.reduce_sum(
            rsum.unsqueeze(2), P3(prob), axis=AX.X), att=[("sV", 1)])
        ph.wt("vector", "sV", v_sum)
        ph.add("vector", lambda v: v.tensor_scalar(rsum, rsum, 1e-30, None,
                                                   ALU.add))
        ph.op("vector", lambda v: v.reciprocal(rcp, rsum), att=[("sV", 1)])
        ph.wt("vector", "sV", v_rcp)
        ph.add("vector", lambda v: v.tensor_tensor(
            P3(prob), P3(prob), bc8(rcp), ALU.mult))
        ph.add("vector", lambda v: v.tensor_copy(wcur, prob))
        for it in range(4):
            ph.op("vector", lambda v: v.reduce_max(
                thr.unsqueeze(2), P3(wcur), axis=AX.X), att=[("sV", 1)])
            ph.wt("vector", "sV", v_thr[it])
            ph.add("vector", lambda v: v.tensor_tensor(
                P3(tmp), P3(wcur), bc8(thr), ALU.is_ge))
            ph.add("vector", lambda v: v.scalar_tensor_tensor(
                wcur, tmp, -2.0, wcur, ALU.mult, ALU.add))
        ph.add("vector", lambda v: v.tensor_tensor(
            P3(tmp), P3(prob), bc8(thr), ALU.is_ge))
        ph.add("vector", lambda v: v.tensor_mul(wcur, prob, tmp))
        ph.op("vector", lambda v: v.reduce_sum(
            rsum.unsqueeze(2), P3(wcur), axis=AX.X), att=[("sV", 1)])
        ph.wt("vector", "sV", v_ssum)
        ph.add("vector", lambda v: v.tensor_scalar(rsum, rsum, 1e-8, None,
                                                   ALU.add))
        ph.op("vector", lambda v: v.reciprocal(rcp, rsum), att=[("sV", 1)])
        ph.wt("vector", "sV", v_rcp2)
        ph.op("vector", lambda v: v.tensor_tensor(
            P3(wcur), P3(wcur), bc8(rcp), ALU.mult), att=[("sV", 1)])
        for tt in range(8):
            ph.wt("vector", "sP1", t_tp[tt])
            ph.op("vector", lambda v, tt=tt: v.tensor_scalar(
                wfts[:, tt * 128: tt * 128 + 128],
                psB[:, (tt % 2) * 512:(tt % 2) * 512 + 128],
                cf[:, 192:193], None, ALU.mult), att=[("sV", 1)])
        for j in range(16):
            ph.wt("vector", "sP1", t_ck[j])
            k, g = j // 2, j % 2
            ph.op("vector", lambda v, k=k, g=g, j=j: v.tensor_copy(
                u1[:, U1_CK + (2 * k + g) * 512:
                   U1_CK + (2 * k + g) * 512 + 512],
                psA[:, (j % 2) * 512:(j % 2) * 512 + 512]), att=[("sV", 1)])

        ph.wt("scalar", "sV", v_sub)
        ph.op("scalar", lambda sc: sc.activation(prob, prob, AF.Exp),
              att=[("sS", 1)])

        for tt in range(8):
            ph.wt("tensor", "sV", v_wf)
            if tt >= 2:
                ph.wt("tensor", "sV", v_wft[tt - 2])
            ph.op("tensor", lambda t, tt=tt: t.transpose(
                psB[:, (tt % 2) * 512:(tt % 2) * 512 + 128],
                sc32[:, SC_WFT + tt * 128: SC_WFT + tt * 128 + 128],
                cf[:, 0:128]), att=[("sP1", 1)])
        for j in range(16):
            k, g = j // 2, j % 2
            ph.wt("tensor", "sV", v_wft[7])
            if j >= 2:
                ph.wt("tensor", "sV", v_ck[j - 2])
            ph.op("tensor", lambda t, k=k, g=g, j=j: t.matmul(
                psA[:, (j % 2) * 512:(j % 2) * 512 + 512],
                mxb[:, k * 128: k * 128 + 128],
                wfts[:, g * 512: g * 512 + 512], start=True, stop=True,
                skip_group_check=True), att=[("sP1", 1)])
        return v_ck[-1]

    def emit_gateup(tag, ck_done):
        ph = Ph("gu_" + tag)
        order = []
        for k in range(8):
            for c2 in range(8):
                order.append(("wg", k * 8 + c2, FC))
                order.append(("wu", k * 8 + c2, FC))
        recs = sched_strips(ph, order)
        steps = []
        for k in range(8):
            for c2 in range(8):
                for g in range(2):
                    steps.append(dict(k=k, c=c2, g=g,
                                      rg=recs[(k * 8 + c2) * 2],
                                      ru=recs[(k * 8 + c2) * 2 + 1],
                                      par=len(steps) % 2))
        for st in steps:
            st["mm"] = inc("sP1")
            if st["g"] == 1:
                inc("sWf", 2)
        for st in steps:
            st["silu"] = inc("sS")
        for st in steps:
            st["ev"] = inc("sV")
        ph.wt("tensor", "sV", ck_done)
        for i, st in enumerate(steps):
            if st["g"] == 0:
                ph.wt("tensor", "sW", st["ru"]["sW_after"])
            if i >= 2:
                ph.wt("tensor", "sV", steps[i - 2]["ev"])
            pg = GEN[2 * st["par"]]
            pu = GEN[2 * st["par"] + 1]
            for kc in range(FC):
                ph.add("tensor", lambda t, pg=pg, w=wslice(st["rg"], kc),
                       h=hslc(kc, st["g"]), k=kc: t.matmul(
                           pg, w, h, start=(k == 0), stop=(k == FC - 1),
                           skip_group_check=True))
                last = (kc == FC - 1)
                att = ([("sP1", 1)] + ([("sWf", 2)] if st["g"] == 1 else [])) \
                    if last else ()
                ph.op("tensor", lambda t, pu=pu, w=wslice(st["ru"], kc),
                      h=hslc(kc, st["g"]), k=kc: t.matmul(
                          pu, w, h, start=(k == 0), stop=(k == FC - 1),
                          skip_group_check=True), att=att,
                      selfwait=("sP1", st["mm"]) if last else None)
        for i, st in enumerate(steps):
            ph.wt("scalar", "sP1", st["mm"])
            if i >= 2:
                ph.wt("scalar", "sV", steps[i - 2]["ev"])
            sg = scbf[:, st["par"] * 1024: st["par"] * 1024 + 512]
            ph.op("scalar", lambda sc, sg=sg, pg=GEN[2 * st["par"]]:
                  sc.activation(sg, pg, AF.Silu), att=[("sS", 1)])
        for i, st in enumerate(steps):
            ph.wt("vector", "sP1", st["mm"])
            sg = scbf[:, st["par"] * 1024: st["par"] * 1024 + 512]
            ub = scbf[:, st["par"] * 1024 + 512: st["par"] * 1024 + 1024]
            ckt = u1[:, U1_CK + (2 * st["k"] + st["g"]) * 512:
                     U1_CK + (2 * st["k"] + st["g"]) * 512 + 512]
            ph.add("vector", lambda v, ub=ub, pu=GEN[2 * st["par"] + 1],
                   ckt=ckt: v.tensor_tensor(ub, pu, ckt, ALU.mult))
            ph.wt("vector", "sS", st["silu"])
            acc = u1[:, U1_ACC + st["c"] * TOK + st["g"] * GW:
                     U1_ACC + st["c"] * TOK + st["g"] * GW + 512]
            if st["k"] == 0:
                ph.op("vector", lambda v, acc=acc, sg=sg, ub=ub:
                      v.tensor_mul(acc, sg, ub), att=[("sV", 1)])
            else:
                ph.add("vector", lambda v, sg=sg, ub=ub:
                       v.tensor_mul(ub, sg, ub))
                ph.op("vector", lambda v, acc=acc, ub=ub:
                      v.tensor_add(acc, acc, ub), att=[("sV", 1)])
        return {g: max(st["ev"] for st in steps if st["g"] == g)
                for g in range(2)}

    def emit_shared(tag):
        ph = Ph("sh_" + tag)
        order = []
        for mi in range(16):
            order.append(("wsg", mi, FC))
            order.append(("wsu", mi, FC))
        recs = sched_strips(ph, order)
        steps = []
        for mi in range(16):
            for g in range(2):
                steps.append(dict(mi=mi, g=g, rg=recs[mi * 2],
                                  ru=recs[mi * 2 + 1], par=len(steps) % 2))
        for st in steps:
            st["mm"] = inc("sP1")
            if st["g"] == 1:
                inc("sWf", 2)
        for st in steps:
            st["silu"] = inc("sS")
        for st in steps:
            st["ev"] = inc("sV")
        for i, st in enumerate(steps):
            if st["g"] == 0:
                ph.wt("tensor", "sW", st["ru"]["sW_after"])
            if i >= 2:
                ph.wt("tensor", "sV", steps[i - 2]["ev"])
            pg = GEN[2 * st["par"]]
            pu = GEN[2 * st["par"] + 1]
            for kc in range(FC):
                ph.add("tensor", lambda t, pg=pg, w=wslice(st["rg"], kc),
                       h=hslc(kc, st["g"]), k=kc: t.matmul(
                           pg, w, h, start=(k == 0), stop=(k == FC - 1),
                           skip_group_check=True))
                last = (kc == FC - 1)
                att = ([("sP1", 1)] + ([("sWf", 2)] if st["g"] == 1 else [])) \
                    if last else ()
                ph.op("tensor", lambda t, pu=pu, w=wslice(st["ru"], kc),
                      h=hslc(kc, st["g"]), k=kc: t.matmul(
                          pu, w, h, start=(k == 0), stop=(k == FC - 1),
                          skip_group_check=True), att=att,
                      selfwait=("sP1", st["mm"]) if last else None)
        for i, st in enumerate(steps):
            ph.wt("scalar", "sP1", st["mm"])
            if i >= 2:
                ph.wt("scalar", "sV", steps[i - 2]["ev"])
            sg = scbf[:, st["par"] * 1024: st["par"] * 1024 + 512]
            ph.op("scalar", lambda sc, sg=sg, pg=GEN[2 * st["par"]]:
                  sc.activation(sg, pg, AF.Silu), att=[("sS", 1)])
        for i, st in enumerate(steps):
            ph.wt("vector", "sP1", st["mm"])
            sg = scbf[:, st["par"] * 1024: st["par"] * 1024 + 512]
            ub = scbf[:, st["par"] * 1024 + 512: st["par"] * 1024 + 1024]
            ph.add("vector", lambda v, ub=ub, pu=GEN[2 * st["par"] + 1]:
                   v.tensor_copy(ub, pu))
            ph.wt("vector", "sS", st["silu"])
            dst = u2[:, st["mi"] * TOK + st["g"] * GW:
                     st["mi"] * TOK + st["g"] * GW + 512]
            ph.op("vector", lambda v, dst=dst, sg=sg, ub=ub:
                  v.tensor_mul(dst, sg, ub), att=[("sV", 1)])
        return {g: max(st["ev"] for st in steps if st["g"] == g)
                for g in range(2)}

    def emit_down(tag, gu_last, sh_last):
        ph = Ph("dn_" + tag)
        order = []
        for mi in range(16):
            order.append(("wd", mi, 8))
            order.append(("wsd", mi, FC))
        recs = sched_strips(ph, order)
        steps = []
        for mi in range(16):
            for g in range(2):
                steps.append(dict(mi=mi, g=g, rd=recs[mi * 2],
                                  rs=recs[mi * 2 + 1], gen=len(steps) % 4))
        for st in steps:
            st["mm"] = inc("sP1")
            if st["g"] == 1:
                inc("sWf", 2)
        for st in steps:
            st["ev"] = inc("sV")
        src_waited = [False, False]
        for i, st in enumerate(steps):
            if not src_waited[st["g"]]:
                ph.wt("tensor", "sV",
                      max(gu_last[st["g"]], sh_last[st["g"]]))
                src_waited[st["g"]] = True
            if st["g"] == 0:
                ph.wt("tensor", "sW", st["rs"]["sW_after"])
            if i >= 4:
                ph.wt("tensor", "sV", steps[i - 4]["ev"])
            gen = GEN[st["gen"]]
            for kc in range(8):
                accs = u1[:, U1_ACC + kc * TOK + st["g"] * GW:
                          U1_ACC + kc * TOK + st["g"] * GW + 512]
                ph.add("tensor", lambda t, gen=gen, w=wslice(st["rd"], kc),
                       a=accs, k=kc: t.matmul(
                           gen, w, a, start=(k == 0), stop=False,
                           skip_group_check=True))
            for kc in range(FC):
                shs = u2[:, kc * TOK + st["g"] * GW:
                         kc * TOK + st["g"] * GW + 512]
                last = (kc == FC - 1)
                att = ([("sP1", 1)] + ([("sWf", 2)] if st["g"] == 1 else [])) \
                    if last else ()
                ph.op("tensor", lambda t, gen=gen, w=wslice(st["rs"], kc),
                      sh=shs, k=kc: t.matmul(
                          gen, w, sh, start=False, stop=(k == FC - 1),
                          skip_group_check=True), att=att,
                      selfwait=("sP1", st["mm"]) if last else None)
        for st in steps:
            ph.wt("vector", "sP1", st["mm"])
            xs = xb[:, st["mi"] * TOK + st["g"] * GW:
                    st["mi"] * TOK + st["g"] * GW + 512]
            ph.op("vector", lambda v, xs=xs, gen=GEN[st["gen"]]:
                  v.tensor_add(xs, xs, gen), att=[("sV", 1)])

    def emit_mask_reload():
        ph = Ph("mreload")
        ph.wt("sync", "sP1", C["sP1"])
        inc("sH", 16)
        ph.op("sync", lambda s: s.dma_start(
            out=u2[:, U2_MASK:U2_MASK + 4 * MW]
            .rearrange("p (r w) -> p r w", w=MW),
            in_=d_msk.rearrange("(r p) w -> p r w", p=128)), att=[("sH", 16)])
        ph.wt("sync", "sH", C["sH"])

    sub = 0
    if sub < N_SUB:
        pla = emit_ln("a", with_router=False)
        qa = emit_qkv("a", "wqa", rope=True, bias=False, pl=pla, ccsem="sC")
        if os.environ.get("ATH_STOP") == "qkv":
            ph = Ph("qkvbar")
            ph.wt("sync", "sC", 1)
            ph.wt("sync", "sV", C["sV"])
            ph.wt("sync", "sES", C["sES"])
        else:
            fa = emit_attention("a", qa)
            emit_outproj("a", "woa", bias=False, fins=fa)
        sub += 1
    STOP = os.environ.get("ATH_STOP", "")
    if sub < N_SUB:
        pl = emit_ln("f1", with_router=True)
        if STOP != "lnf1":
            ck1 = emit_router_ep("f1", pl["router_done"])
            if STOP != "rtf1":
                gu1 = emit_gateup("f1", ck1)
                if STOP != "guf1":
                    sh1 = emit_shared("f1")
                    if STOP != "shf1":
                        emit_down("f1", gu1, sh1)
        sub += 1
    if sub < N_SUB:
        emit_mask_reload()
        plb = emit_ln("b", with_router=False)
        qb = emit_qkv("b", "wqb", rope=False, bias=True, pl=plb, ccsem="sC2")
        fb = emit_attention("b", qb)
        emit_outproj("b", "wob", bias=True, fins=fb)
        sub += 1
    if sub < N_SUB:
        pl = emit_ln("f2", with_router=True)
        ck2 = emit_router_ep("f2", pl["router_done"])
        gu2 = emit_gateup("f2", ck2)
        sh2 = emit_shared("f2")
        emit_down("f2", gu2, sh2)
        sub += 1

    assert C["sWf"] == wf_n[0], (C["sWf"], wf_n[0])

    ph = Ph("store")
    ph.wt("sync", "sV", C["sV"])
    ph.wt("sync", "sS", C["sS"])
    inc("sH", 16)
    ph.op("sync", lambda s: s.dma_start(
        out=d_out.rearrange("(k p) t -> p k t", p=128),
        in_=xb[:].rearrange("p (k t) -> p k t", t=TOK)), att=[("sH", 16)])
    if os.environ.get("ATH_DUMP"):
        dmp = {
            "dbg_hb": (nc.dram_tensor("dbg_hb", [128, FC * TOK], BF16,
                                      kind="ExternalOutput").ap(), hb[:]),
            "dbg_u1": (nc.dram_tensor("dbg_u1", [128, FC * TOK], BF16,
                                      kind="ExternalOutput").ap(), u1[:]),
            "dbg_u2": (nc.dram_tensor("dbg_u2", [128, FC * TOK], BF16,
                                      kind="ExternalOutput").ap(), u2[:]),
            "dbg_sc": (nc.dram_tensor("dbg_sc", [128, 2048], F32,
                                      kind="ExternalOutput").ap(),
                       sc32[:, 1536:3584]),
            "dbg_cc": (nc.dram_tensor("dbg_cc", [16384, TOK], BF16,
                                      kind="ExternalOutput").ap(), cc_out[:]),
            "dbg_mx": (nc.dram_tensor("dbg_mx", [128, 1024], BF16,
                                      kind="ExternalOutput").ap(),
                       scbf[:, 1024:2048]),
            "dbg_sl": (nc.dram_tensor("dbg_sl", [128, 1536], F32,
                                      kind="ExternalOutput").ap(),
                       sc32[:, 0:1536]),
        }
        for name, (dd, ss) in dmp.items():
            inc("sH", 16)
            ph.op("sync", lambda s, dd=dd, ss=ss: s.dma_start(out=dd, in_=ss),
                  att=[("sH", 16)])
    ph.wt("sync", "sH", C["sH"])

    for p in phases:
        with nc.Block(p.name) as blk:
            for e in ENGS:
                fns = p.ops[e]
                if not fns:
                    continue
                def run(engine, fns=fns):
                    for f in fns:
                        f(engine)
                getattr(blk, e)(run)
    ctx.close()
    return nc


def kernel(**inputs):
    maps = host_inputs(inputs)
    nc = build_nc()
    res = run_bass_kernel_spmd(nc, maps, core_ids=list(range(8)))
    out = np.empty((B, S, E), np.float32)
    for c in range(8):
        beta, lane = c // 4, c % 4
        out[beta, lane::4, :] = np.asarray(res.results[c]["out"], np.float32).T
    return out
